# version 13
# baseline (speedup 1.0000x reference)
import numpy as np
import ml_dtypes
import concourse.bass as bass
import concourse.mybir as mybir
from concourse.bass_utils import run_bass_kernel_spmd

F32 = mybir.dt.float32
BF16 = mybir.dt.bfloat16
AF = mybir.ActivationFunctionType
ALU = mybir.AluOpType
AX = mybir.AxisListType

B, T, D = 4, 4096, 2048
H, LAT, RD, HD = 16, 512, 64, 128
E, I = 16, 1024
TQ = T // 2
NEG = -1.0e30

DB = D // 128
SB = T // 128
TB = TQ // 128
LB = LAT // 128
IB = I // 128
NLANES = 8


def cs(i, w):
    return slice(i * w, (i + 1) * w)


class Op:
    __slots__ = ("eng", "fn", "deps", "dma", "lane", "need_inc", "idx", "tag",
                 "serial")

    def __init__(self, eng, fn, dma, lane, tag, serial):
        self.eng, self.fn, self.dma, self.lane = eng, fn, dma, lane
        self.deps = []
        self.need_inc = False
        self.idx = None
        self.tag = tag
        self.serial = serial


class Reg:
    __slots__ = ("writers", "readers", "name")

    def __init__(self, name=""):
        self.writers = {}
        self.readers = {}
        self.name = name


class Sched:
    COMPUTE = {"tensor": "t1", "vector": "v1", "scalar": "a1", "gpsimd": "g1",
               "sync": "s1"}

    def __init__(self):
        self.ops = []
        self.lane_ctr = {}

    def stream(self, o):
        if o.dma:
            return f"q_{o.eng}_{o.lane}"
        return self.COMPUTE[o.eng]

    def op(self, eng, fn, reads=(), writes=(), par_writes=(), dma=False, tag=""):
        lane = None
        if dma:
            c = self.lane_ctr.get(eng, 0)
            self.lane_ctr[eng] = c + 1
            lane = c % NLANES
        o = Op(eng, fn, dma, lane, tag, len(self.ops))
        st = self.stream(o)
        deps = {}

        def add(d):
            if d is not None and d is not o and (d.serial not in deps):
                deps[d.serial] = d

        for r in reads:
            for w in r.writers.values():
                add(w)
            r.readers[st] = o
        for r in writes:
            for w in r.writers.values():
                add(w)
            for rd in r.readers.values():
                add(rd)
            r.writers = {st: o}
            r.readers = {}
        for r in par_writes:
            for rd in r.readers.values():
                add(rd)
            r.writers[st] = o
        o.deps = [deps[k] for k in sorted(deps)]
        self.ops.append(o)
        return o

    @staticmethod
    def alias(*olds, name=""):
        r = Reg(name)
        for old in olds:
            for d in (old.writers, old.readers):
                for st, o in d.items():
                    cur = r.readers.get(st)
                    if cur is None or cur.serial < o.serial:
                        r.readers[st] = o
        return r

    def skip_edge(self, d, o):
        return (not d.dma) and d.eng == "tensor" and o.eng == "tensor"

    def finalize(self):
        for o in self.ops:
            for d in o.deps:
                if not self.skip_edge(d, o):
                    d.need_inc = True
        counts = {}
        for o in self.ops:
            if o.need_inc or o.dma:
                s = self.stream(o)
                counts[s] = counts.get(s, 0) + (16 if o.dma else 1)
                o.idx = counts[s]
        self.totals = counts

    def emit(self, block, sems):
        by_eng = {}
        for o in self.ops:
            by_eng.setdefault(o.eng, []).append(o)
        sched = self

        def run(eng_name):
            ops = by_eng.get(eng_name, [])

            def body(eng):
                hiwater = {}
                for o in ops:
                    for d in o.deps:
                        if sched.skip_edge(d, o):
                            continue
                        s = sched.stream(d)
                        if d.idx is None:
                            raise RuntimeError(f"no idx: {d.tag} -> {o.tag}")
                        if hiwater.get(s, 0) < d.idx:
                            hiwater[s] = d.idx
                            eng.wait_ge(sems[s], d.idx)
                    ins = o.fn(eng)
                    if o.idx is not None:
                        ins.then_inc(sems[sched.stream(o)], 16 if o.dma else 1)
                if eng_name == "sync":
                    for s, tot in sorted(sched.totals.items()):
                        if s.startswith("q_"):
                            eng.wait_ge(sems[s], tot)
            return body

        for name in ("sync", "scalar", "vector", "tensor", "gpsimd"):
            getattr(block, name)(run(name))


def build_program(taps=()):
    nc = bass.Bass("TRN2", target_bir_lowering=False)
    S = Sched()

    def din(name, shape, dt=F32):
        return nc.dram_tensor(name, shape, dt, kind="ExternalInput").ap()

    x_b = din("x_b", [T, D])
    x_q = din("x_q", [TQ, D])
    wqd = din("wqd", [D, LAT]);  wqu = din("wqu", [LAT, D])
    wqr = din("wqr", [LAT, H * RD]);  wkvd = din("wkvd", [D, LAT])
    wku = din("wku", [LAT, D]);  wvu = din("wvu", [LAT, D])
    wkr = din("wkr", [D, RD]);  wo = din("wo", [D, D])
    gate_w = din("gate_w", [E, D]);  ebias_bc = din("ebias_bc", [128, E])
    w_gate = din("w_gate", [E, I, D]);  w_up = din("w_up", [E, I, D])
    w_down = din("w_down", [E, D, I])
    sw_gate = din("sw_gate", [I, D]);  sw_up = din("sw_up", [I, D])
    sw_down = din("sw_down", [D, I])
    cosq = din("cosq", [RD, TQ], BF16);  sinq = din("sinq", [RD, TQ], BF16)
    cosk = din("cosk", [RD, T], BF16);  sink = din("sink", [RD, T], BF16)
    maskin = din("maskin", [128, 8 * 512], BF16)
    ident = din("ident", [128, 128])
    ones_col = din("ones_col", [128, 1], BF16)
    ones_row = din("ones_row", [1, 128], BF16)

    out = nc.dram_tensor("out", [TQ, D], F32, kind="ExternalOutput").ap()

    def scr(name, shape, dt=BF16):
        return nc.dram_tensor(name, shape, dt, kind="Internal").ap()

    wqd_s = scr("wqd_s", [D, LAT]);  wkvd_s = scr("wkvd_s", [D, LAT])
    wkr_s = scr("wkr_s", [D, RD]);  wqu_s = scr("wqu_s", [LAT, D])
    wqr_s = scr("wqr_s", [LAT, H * RD]);  wku_s = scr("wku_s", [LAT, D])
    wvu_s = scr("wvu_s", [LAT, D]);  wo_s = scr("wo_s", [D, D])
    gw_s = scr("gw_s", [E, D])
    wg_s = scr("wg_s", [E + 1, I, D]);  wu_s = scr("wu_s", [E + 1, I, D])
    wd_s = scr("wd_s", [E + 1, D, I])
    hT_s = scr("hT_s", [E + 1, I, TQ])
    x2_s = scr("x2_s", [TQ, D], F32)
    y0_s = scr("y0_s", [TQ, D // 2], F32)

    tap_outs = {}
    for name, shape in taps:
        tap_outs[name] = nc.dram_tensor("tap_" + name, shape, F32,
                                        kind="ExternalOutput").ap()

    from contextlib import ExitStack
    with ExitStack() as ctx:
        sbuf = lambda n, sh, dt: ctx.enter_context(nc.sbuf_tensor(n, sh, dt))
        psum = lambda n, sh, dt=F32: ctx.enter_context(nc.psum_tensor(n, sh, dt))

        big16 = sbuf("big16", [128, 16 * 2048], BF16)
        ckvA = sbuf("ckvA", [128, 8 * 2048], BF16)
        qlB = sbuf("qlB", [128, 4 * 2048], BF16)
        wA = sbuf("wA", [128, 16 * 2048], BF16)
        fS = sbuf("fS", [128, 2 * 2048], F32)
        ropetmp = sbuf("ropetmp", [64, 512], F32)
        rsrow = None
        fT = sbuf("fT", [128, 160], F32)
        id_f = sbuf("id_f", [128, 128], F32)
        onec = sbuf("onec", [128, 1], BF16)
        oner = sbuf("oner", [1, 128], BF16)
        gwT = sbuf("gwT", [128, DB * E], BF16)
        ebias_t = sbuf("ebias_t", [128, E], F32)
        krT = sbuf("krT", [RD, T], BF16)
        bcast_sb = sbuf("bcast_sb", [128, 512], F32)
        cw_all = sbuf("cw_all", [128, TB * E], F32)

        pX = psum("pX", [128, 2048])
        pA = pX[:, 0:512]
        pB = pX[:, 512:1024]
        pC = pX[:, 1024:1536]
        pD = pX[:, 1536:2048]
        pO = psum("pO", [128, 2048])

        sems = {}
        for s in ("t1", "v1", "a1", "g1", "s1"):
            sems[s] = ctx.enter_context(nc.semaphore(s))
        for q in ("sync", "gpsimd", "scalar"):
            for ln in range(NLANES):
                key = f"q_{q}_{ln}"
                sems[key] = ctx.enter_context(nc.semaphore(key))

        rsrow = ropetmp[0:1, :].bitcast(BF16)[0:1, 0:512]
        accF = big16[:].bitcast(F32)
        fQ = qlB[:].bitcast(F32)
        VTf = wA[:].bitcast(F32)
        fSb = fS[:].bitcast(BF16)

        WQD = lambda db: wA[:, db * 512:(db + 1) * 512]
        WKVD = lambda db: wA[:, 8192 + db * 512:8192 + (db + 1) * 512]
        WKR = lambda db: wA[:, 16384 + db * 64:16384 + (db + 1) * 64]
        XNBT = lambda db: wA[:, 17408 + db * 512:17408 + (db + 1) * 512]
        COSK = wA[0:64, 25600:29696]
        SINK = wA[64:128, 25600:29696]
        MASK = lambda rb: wA[:, rb * 512:(rb + 1) * 512]
        KT = wA[:, 4096:8192]
        VT32 = VTf[:, 4096:8192]
        VS = wA[:, 16384:20480]
        QT = wA[:, 20480:22528]
        QRT = wA[0:64, 22528:24576]
        COSQ = wA[0:64, 24576:26624]
        SINQ = wA[64:128, 24576:26624]
        PROBS = lambda i: wA[:, 26624 + i * 512:26624 + (i + 1) * 512]
        WHEAD = lambda sl: 28672 + sl * 1792
        WO = lambda db: wA[:, db * 2048:(db + 1) * 2048]
        WGUP = lambda sl: 8192 * sl
        WDT = wA[:, 0:8192]
        HSTB = wA[:, 8192:24576]

        r_const = Reg("const")
        r_const_eps = Reg("eps")
        r_fT = Reg("fT")
        r_fTa = [Reg("fTa0"), Reg("fTa1")]
        r_big = Reg("big16")
        r_ckvA = Reg("ckvA")
        r_qlB = Reg("qlB")
        r_krT = Reg("krT")
        r_gwT = Reg("gwT")
        r_bc = Reg("bcast")
        r_cw = Reg("cw")
        r_stage = [Reg(f"fS{i}") for i in range(2)]
        r_rope = Reg("ropetmp")
        r_rs = r_rope
        r_pA, r_pB, r_pC, r_pD = Reg("pA"), Reg("pB"), Reg("pC"), Reg("pD")
        r_pO = [Reg(f"pO{i}") for i in range(4)]
        r_wqd, r_wkvd, r_wkr = Reg("wqd"), Reg("wkvd"), Reg("wkr")
        r_xnbT, r_tabk = Reg("xnbT"), Reg("tabk")
        R_cast = {}
        r_hTs = [Reg(f"hTs{e}") for e in range(E + 1)]
        r_x2s = Reg("x2s")
        r_y0s = Reg("y0s")
        r_out = Reg("out")

        POq = [pO[:, cs(i, 512)] for i in range(4)]

        S.op("vector", lambda E_: E_.memset(fT[:, 158:159], 1e-5),
             writes=[r_const_eps], tag="p0.eps")

        def load(dst, src, rw, rr=(), q="sync", tag="ld", par=True):
            kw = dict(reads=list(rr), dma=True, tag=tag)
            if par:
                kw["par_writes"] = [rw]
            else:
                kw["writes"] = [rw]
            return S.op(q, lambda E_, d=dst, s=src: E_.dma_start(out=d, in_=s),
                        **kw)

        load(id_f[:], ident[:], r_const)
        load(onec[:], ones_col[:], r_const)
        load(oner[:], ones_row[:], r_const)
        load(ebias_t[:], ebias_bc[:], r_const)
        load(COSK, cosk[:], r_tabk)
        load(SINK, sink[:], r_tabk)

        def cast(dst, src, name, tag, rr=()):
            r = Reg("c_" + name)
            R_cast[name] = r
            S.op("gpsimd", lambda E_, d=dst, s=src: E_.dma_start(out=d, in_=s),
                 reads=list(rr), par_writes=[r], dma=True, tag=tag)

        cast(wqd_s[:], wqd[:], "wqd", "c.wqd")
        cast(wkvd_s[:], wkvd[:], "wkvd", "c.wkvd")
        cast(wkr_s[:], wkr[:], "wkr", "c.wkr")
        cast(wqu_s[:], wqu[:], "wqu", "c.wqu")
        cast(wqr_s[:], wqr[:], "wqr", "c.wqr")
        cast(wku_s[:], wku[:], "wku", "c.wku")
        cast(wvu_s[:], wvu[:], "wvu", "c.wvu")
        cast(gw_s[:], gate_w[:], "gw", "c.gw")


        def ln_bst(srcs, rr, area, tag):
            a = 29 * area
            ra = r_fTa[area]
            n = 0
            for ap, nch in srcs:
                for i in range(nch):
                    S.op("vector", lambda E_, ap=ap, i=i, n=n, a=a: E_.bn_stats(
                        fT[:, a + 6 * (n + i):a + 6 * (n + i) + 6],
                        ap[:, cs(i, 512)]),
                        reads=list(rr), par_writes=[ra], tag=tag + ".bst")
                n += nch
            return n

        def ln_fin(n, area, tag):
            a = 29 * area
            ra = r_fTa[area]
            S.op("vector", lambda E_, n=n, a=a: E_.bn_aggr(
                fT[:, a + 24:a + 26], fT[:, a:a + 6 * n]),
                reads=[ra], writes=[ra], tag=tag + ".agg")
            S.op("scalar", lambda E_, a=a: E_.activation(
                fT[:, a + 27:a + 28], fT[:, a + 25:a + 26], AF.Sqrt,
                bias=fT[:, 158:159]),
                reads=[ra, r_const_eps], writes=[ra], tag=tag + ".sqrt")
            S.op("vector", lambda E_, a=a: E_.reciprocal(
                fT[:, a + 28:a + 29], fT[:, a + 27:a + 28]),
                reads=[ra], writes=[ra], tag=tag + ".rcp")

        def ln_stats2(srcs, rr, area, tag):
            ln_fin(ln_bst(srcs, rr, area, tag), area, tag)

        def ln_apply2(dst, src, r_src, r_dst, area, tag, par=False):
            a = 29 * area
            kw = dict(writes=[r_dst]) if not par else dict(par_writes=[r_dst])
            S.op("vector", lambda E_, d=dst, s=src, a=a: E_.tensor_scalar(
                d, s, fT[:, a + 24:a + 25], fT[:, a + 28:a + 29],
                ALU.subtract, ALU.mult),
                reads=list(r_src) + [r_fTa[area]], **kw, tag=tag + ".aff")

        r_stq = [Reg("fQ0"), Reg("fQ1")]
        stage4 = [fS[:, 0:2048], fS[:, 2048:4096], fQ[:, 0:2048], fQ[:, 2048:4096]]
        r_stage4 = [r_stage[0], r_stage[1], r_stq[0], r_stq[1]]
        def p1a_tail(tb):
            st = r_stage4[tb % 4]
            stage = stage4[tb % 4]
            ln_fin(4, tb % 2, "p1.ln")
            ln_apply2(stage, stage, [st], st, tb % 2, "p1.ln")
            for db in range(DB):
                pr = [r_pA, r_pB][db % 2]
                pp = [pA, pB][db % 2]
                S.op("tensor", lambda E_, s=stage[:, cs(db, 128)], pp=pp:
                     E_.transpose(pp[:, 0:128], s, id_f[:]),
                    reads=[st, r_const], writes=[pr], tag="p1.tq")
                S.op("scalar", lambda E_, db=db, tb=tb, pp=pp: E_.copy(
                    big16[:, db * 2048 + tb * 128:db * 2048 + (tb + 1) * 128],
                    pp[:, 0:128]),
                    reads=[pr], par_writes=[r_big], tag="p1.tqe")

        for tb in range(TB):
            st = r_stage4[tb % 4]
            stage = stage4[tb % 4]
            load(stage, x_q[cs(tb, 128), :], st, q="scalar", tag="p1.ldxq",
                 par=False)
            ln_bst([(stage, 4)], [st], tb % 2, "p1.ln")
            if tb > 0:
                p1a_tail(tb - 1)
        p1a_tail(TB - 1)

        for db in range(DB):
            load(WQD(db), wqd_s[cs(db, 128), :], r_wqd, rr=[R_cast["wqd"]],
                 tag="ld.wqd")
            load(WKVD(db), wkvd_s[cs(db, 128), :], r_wkvd, rr=[R_cast["wkvd"]],
                 tag="ld.wkvd")
            load(WKR(db), wkr_s[cs(db, 128), :], r_wkr, rr=[R_cast["wkr"]],
                 tag="ld.wkr")

        CKVT = lambda lb: ckvA[:, lb * 4096:(lb + 1) * 4096]
        p1b_pending = [None]
        for sc in range(8):
            def p1b_tail(tile):
                st = r_stage4[tile % 4]
                stage = stage4[tile % 4]
                j = tile % 4
                ln_fin(4, tile % 2, "p1b.ln")
                ln_apply2(stage, stage, [st], st, tile % 2, "p1b.ln")
                for db in range(DB):
                    pr = [r_pA, r_pB][db % 2]
                    pp = [pA, pB][db % 2]
                    S.op("tensor", lambda E_, s=stage[:, cs(db, 128)], pp=pp:
                         E_.transpose(pp[:, 0:128], s, id_f[:]),
                        reads=[st, r_const], writes=[pr], tag="p1b.t")
                    S.op("scalar", lambda E_, db=db, j=j, pp=pp: E_.copy(
                        XNBT(db)[:, cs(j, 128)], pp[:, 0:128]),
                        reads=[pr], par_writes=[r_xnbT], tag="p1b.te")

            for j in range(4):
                tile = sc * 4 + j
                st = r_stage4[tile % 4]
                stage = stage4[tile % 4]
                load(stage, x_b[cs(tile, 128), :], st, tag="p1.ldxb", par=False)
                ln_bst([(stage, 4)], [st], tile % 2, "p1b.ln")
                if p1b_pending[0] is not None:
                    p1b_tail(p1b_pending[0])
                p1b_pending[0] = tile
            p1b_tail(p1b_pending[0])
            p1b_pending[0] = None
            for lb in range(LB):
                pr = [r_pC, r_pD][lb % 2]
                pp = [pC, pD][lb % 2]
                for db in range(DB):
                    S.op("tensor", lambda E_, lb=lb, db=db, pp=pp: E_.matmul(
                        pp[:], WKVD(db)[:, cs(lb, 128)], XNBT(db),
                        start=(db == 0), stop=(db == DB - 1)),
                        reads=[r_wkvd, r_xnbT], writes=[pr], tag="p1.ckv")
                S.op("scalar", lambda E_, lb=lb, sc=sc, pp=pp: E_.copy(
                    CKVT(lb)[:, cs(sc, 512)], pp[:]),
                    reads=[pr], par_writes=[r_ckvA], tag="p1.ckve")
            for db in range(DB):
                S.op("tensor", lambda E_, db=db: E_.matmul(
                    pO[0:64, 0:512], WKR(db), XNBT(db),
                    start=(db == 0), stop=(db == DB - 1)),
                    reads=[r_wkr, r_xnbT], writes=[r_pO[0]], tag="p1.kr")
            S.op("vector", lambda E_, sc=sc: E_.tensor_tensor(
                bcast_sb[0:64, :], pO[0:64, 0:512], COSK[:, cs(sc, 512)],
                ALU.mult), reads=[r_pO[0], r_tabk], writes=[r_bc], tag="p1.kr1")
            S.op("vector", lambda E_, sc=sc: E_.tensor_tensor(
                ropetmp[0:32, :], pO[32:64, 0:512],
                SINK[0:32, cs(sc, 512)], ALU.mult),
                reads=[r_pO[0], r_tabk], writes=[r_rope], tag="p1.kr2")
            S.op("vector", lambda E_, sc=sc: E_.tensor_tensor(
                ropetmp[32:64, :], pO[0:32, 0:512],
                SINK[32:64, cs(sc, 512)], ALU.mult),
                reads=[r_pO[0], r_tabk], par_writes=[r_rope], tag="p1.kr3")
            S.op("vector", lambda E_, sc=sc: E_.tensor_tensor(
                krT[:, cs(sc, 512)], bcast_sb[0:64, :], ropetmp[:],
                ALU.add),
                reads=[r_bc, r_rope], par_writes=[r_krT], tag="p1.kr4")

        for db in range(DB):
            S.op("sync", lambda E_, db=db: E_.dma_start_transpose(
                gwT[:, cs(db, E)], gw_s[:, cs(db, 128)]),
                reads=[R_cast["gw"]], par_writes=[r_gwT], dma=True, tag="ld.gwT")

        r_qlB = S.alias(r_qlB, r_stq[0], r_stq[1], name="qlB2")
        QLT = lambda lb: qlB[:, lb * 2048:(lb + 1) * 2048]
        for lb in range(LB):
            for tc in range(4):
                k = lb * 4 + tc
                pr = [r_pC, r_pD][k % 2]
                pp = [pC, pD][k % 2]
                for db in range(DB):
                    S.op("tensor", lambda E_, lb=lb, tc=tc, db=db, pp=pp: E_.matmul(
                        pp[:], WQD(db)[:, cs(lb, 128)],
                        big16[:, db * 2048 + tc * 512:db * 2048 + (tc + 1) * 512],
                        start=(db == 0), stop=(db == DB - 1)),
                        reads=[r_wqd, r_big], writes=[pr], tag="p1.ql")
                S.op("scalar", lambda E_, lb=lb, tc=tc, pp=pp: E_.copy(
                    QLT(lb)[:, cs(tc, 512)], pp[:]),
                    reads=[pr], par_writes=[r_qlB], tag="p1.qle")

        def tap_copy(name, src_aps, rr):
            if name not in tap_outs:
                return
            o = tap_outs[name]
            row = 0
            for ap, rows, width in src_aps:
                S.op("vector", lambda E_, ap=ap, rows=rows, width=width:
                     E_.tensor_copy(fS[0:rows, 0:width], ap),
                     reads=list(rr), writes=[r_stage[0], r_stage[1]], tag="tapc")
                S.op("sync", lambda E_, row=row, rows=rows, width=width: E_.dma_start(
                    out=o[row:row + rows, :], in_=fS[0:rows, 0:width]),
                    reads=[r_stage[0]], par_writes=[r_out], dma=True, tag="tapd")
                row += rows

        if "ckvT" in tap_outs:
            tap_copy("ckvT", [(CKVT(lb)[:, 0:4096], 128, 4096) for lb in range(4)],
                     [r_ckvA])
        if "krT" in tap_outs:
            tap_copy("krT", [(krT[:, :], 64, 4096)], [r_krT])
        if "qlT" in tap_outs:
            tap_copy("qlT", [(QLT(lb)[:, :], 128, 2048) for lb in range(4)],
                     [r_qlB])
        if "xnTq" in tap_outs:
            tap_copy("xnTq", [(big16[:, cs(db, 2048)], 128, 2048)
                              for db in range(DB)], [r_big])

        r_mask2 = S.alias(r_wqd, name="mask2")
        r_kT = S.alias(r_wqd, name="kT")
        r_vT = S.alias(r_wkvd, name="vT")
        r_v = S.alias(r_wkr, r_xnbT, name="v")
        r_qT = S.alias(r_xnbT, name="qT")
        r_qrT = S.alias(r_xnbT, name="qrT")
        r_tabq = S.alias(r_xnbT, r_tabk, name="tabq")
        r_probs = [S.alias(r_xnbT, r_tabk, name=f"probs{i}") for i in range(4)]
        r_whead = [S.alias(r_tabk, name=f"whead{i}") for i in range(2)]
        r_avt = S.alias(r_big, name="avt")

        for rb in range(8):
            load(MASK(rb), maskin[:, cs(rb, 512)], r_mask2, tag="ld.mask")
        load(COSQ, cosq[:], r_tabq, tag="ld.cosq")
        load(SINQ, sinq[:], r_tabq, tag="ld.sinq")

        pending_tail = [None]
        for h in range(H):
            if pending_tail[0] is not None:
                pending_tail[0]()
                pending_tail[0] = None
            sl = h % 2
            base = WHEAD(sl)
            rw = r_whead[sl]
            S.op("sync", lambda E_, h=h, base=base: E_.dma_start(
                out=wA[:, base:base + 512].rearrange("p (lb c) -> p lb c", lb=4),
                in_=wqu_s[:, cs(h, 128)].rearrange("(lb p) c -> p lb c", p=128)),
                reads=[R_cast["wqu"]], writes=[rw], dma=True, tag="p2.lwqu")
            S.op("sync", lambda E_, h=h, base=base: E_.dma_start(
                out=wA[:, base + 512:base + 1024].rearrange("p (lb c) -> p lb c", lb=4),
                in_=wku_s[:, cs(h, 128)].rearrange("(lb p) c -> p lb c", p=128)),
                reads=[R_cast["wku"]], par_writes=[rw], dma=True, tag="p2.lwku")
            S.op("scalar", lambda E_, h=h, base=base: E_.dma_start(
                out=wA[:, base + 1024:base + 1536].rearrange("p (lb c) -> p lb c", lb=4),
                in_=wvu_s[:, cs(h, 128)].rearrange("(lb p) c -> p lb c", p=128)),
                reads=[R_cast["wvu"]], par_writes=[rw], dma=True, tag="p2.lwvu")
            S.op("scalar", lambda E_, h=h, base=base: E_.dma_start(
                out=wA[:, base + 1536:base + 1792].rearrange("p (lb c) -> p lb c", lb=4),
                in_=wqr_s[:, cs(h, 64)].rearrange("(lb p) c -> p lb c", p=128)),
                reads=[R_cast["wqr"]], par_writes=[rw], dma=True, tag="p2.lwqr")

            cast(wg_s[h], w_gate[h] if h < E else sw_gate[:], f"wg{h}",
                 f"c.wg{h}", rr=[rw])
            cast(wu_s[h], w_up[h] if h < E else sw_up[:], f"wu{h}",
                 f"c.wu{h}", rr=[rw])
            cast(wd_s[h], w_down[h] if h < E else sw_down[:], f"wd{h}",
                 f"c.wd{h}", rr=[rw])
            if h == 8:
                cast(wo_s[:], wo[:], "wo", "c.wo", rr=[rw])
            if h == 15:
                cast(wg_s[E], sw_gate[:], f"wg{E}", "c.swg", rr=[rw])
                cast(wu_s[E], sw_up[:], f"wu{E}", "c.swu", rr=[rw])
                cast(wd_s[E], sw_down[:], f"wd{E}", "c.swd", rr=[rw])

            WQU_h = lambda lb: wA[:, base + lb * 128:base + (lb + 1) * 128]
            WKU_h = lambda lb: wA[:, base + 512 + lb * 128:base + 512 + (lb + 1) * 128]
            WVU_h = lambda lb: wA[:, base + 1024 + lb * 128:base + 1024 + (lb + 1) * 128]
            WQR_h = lambda lb: wA[:, base + 1536 + lb * 64:base + 1536 + (lb + 1) * 64]

            for tc in range(4):
                pr = [r_pC, r_pD][tc % 2]
                pp = [pC, pD][tc % 2]
                for lb in range(LB):
                    S.op("tensor", lambda E_, w=WQR_h(lb), q=QLT(lb)[:, cs(tc, 512)],
                         lb=lb, pp=pp: E_.matmul(
                        pp[0:64, :], w, q, start=(lb == 0), stop=(lb == LB - 1)),
                        reads=[rw, r_qlB], writes=[pr], tag="p2.qr")
                S.op("vector", lambda E_, tc=tc, pp=pp: E_.tensor_tensor(
                    bcast_sb[0:64, :], pp[0:64, :], COSQ[:, cs(tc, 512)],
                    ALU.mult), reads=[pr, r_tabq], writes=[r_bc], tag="p2.qr1")
                S.op("vector", lambda E_, tc=tc, pp=pp: E_.tensor_tensor(
                    ropetmp[0:32, :], pp[32:64, :], SINQ[0:32, cs(tc, 512)],
                    ALU.mult), reads=[pr, r_tabq], writes=[r_rope],
                    tag="p2.qr2")
                S.op("vector", lambda E_, tc=tc, pp=pp: E_.tensor_tensor(
                    ropetmp[32:64, :], pp[0:32, :], SINQ[32:64, cs(tc, 512)],
                    ALU.mult), reads=[pr, r_tabq], par_writes=[r_rope],
                    tag="p2.qr3")
                S.op("vector", lambda E_, tc=tc: E_.tensor_tensor(
                    QRT[:, cs(tc, 512)], bcast_sb[0:64, :], ropetmp[:],
                    ALU.add), reads=[r_bc, r_rope], par_writes=[r_qrT],
                    tag="p2.qr4")

            for scc in range(8):
                pr = [r_pA, r_pB][scc % 2]
                pp = [pA, pB][scc % 2]
                for lb in range(LB):
                    S.op("tensor", lambda E_, w=WKU_h(lb), c=CKVT(lb)[:, cs(scc, 512)],
                         lb=lb, pp=pp: E_.matmul(
                        pp[:], w, c, start=(lb == 0), stop=(lb == LB - 1)),
                        reads=[rw, r_ckvA], writes=[pr], tag="p2.kt")
                S.op("scalar", lambda E_, scc=scc, pp=pp: E_.copy(
                    KT[:, cs(scc, 512)], pp[:]),
                    reads=[pr], par_writes=[r_kT], tag="p2.kte")
            for scc in range(8):
                pr = [r_pA, r_pB][scc % 2]
                pp = [pA, pB][scc % 2]
                for lb in range(LB):
                    S.op("tensor", lambda E_, w=WVU_h(lb), c=CKVT(lb)[:, cs(scc, 512)],
                         lb=lb, pp=pp: E_.matmul(
                        pp[:], w, c, start=(lb == 0), stop=(lb == LB - 1)),
                        reads=[rw, r_ckvA], writes=[pr], tag="p2.vt")
                S.op("scalar", lambda E_, scc=scc, pp=pp: E_.copy(
                    VT32[:, cs(scc, 512)], pp[:]),
                    reads=[pr], par_writes=[r_vT], tag="p2.vte")
            for sb in range(SB):
                pr = r_pO[sb % 2]
                pp = POq[sb % 2]
                S.op("tensor", lambda E_, sb=sb, pp=pp: E_.transpose(
                    pp[:, 0:128], VT32[:, cs(sb, 128)], id_f[:]),
                    reads=[r_vT, r_const], writes=[pr], tag="p2.vtr")
                S.op("scalar", lambda E_, sb=sb, pp=pp: E_.copy(
                    VS[:, cs(sb, 128)], pp[:, 0:128]),
                    reads=[pr], par_writes=[r_v], tag="p2.vtre")
            scale = float(np.float32(1.0) / np.sqrt(np.float32(HD + RD)))
            for tc in range(4):
                pr = [r_pA, r_pB][tc % 2]
                pp = [pA, pB][tc % 2]
                for lb in range(LB):
                    S.op("tensor", lambda E_, w=WQU_h(lb), q=QLT(lb)[:, cs(tc, 512)],
                         lb=lb, pp=pp: E_.matmul(
                        pp[:], w, q, start=(lb == 0), stop=(lb == LB - 1)),
                        reads=[rw, r_qlB], writes=[pr], tag="p2.qt")
                S.op("scalar", lambda E_, tc=tc, pp=pp, scale=scale: E_.activation(
                    QT[:, cs(tc, 512)], pp[:], AF.Copy, scale=scale),
                    reads=[pr], par_writes=[r_qT], tag="p2.qte")
            for qt in range(4):
                ext = 8 * (qt + 1)
                st_regs = [r_pA, r_pB, r_pO[0], r_pO[1]]
                st_ps = [pA[:], pB[:], POq[0], POq[1]]

                def av_rs(sb, ext=ext):
                    slot = sb % 4
                    S.op("tensor", lambda E_, sb=sb, slot=slot: E_.matmul(
                        pC[:], VS[:, cs(sb, 128)], PROBS(slot),
                        start=(sb == 0), stop=(sb == ext - 1)),
                        reads=[r_v, r_probs[slot]],
                        writes=[r_pC] if sb == 0 else [],
                        par_writes=[] if sb == 0 else [r_pC], tag="p2.av")
                    S.op("tensor", lambda E_, sb=sb, slot=slot: E_.matmul(
                        pD[0:1, :], onec[:], PROBS(slot),
                        start=(sb == 0), stop=(sb == ext - 1)),
                        reads=[r_const, r_probs[slot]],
                        writes=[r_pD] if sb == 0 else [],
                        par_writes=[] if sb == 0 else [r_pD], tag="p2.rs")

                for sb in range(ext):
                    slot = sb % 4
                    prS, ppS = st_regs[slot], st_ps[slot]
                    S.op("tensor", lambda E_, sb=sb, qt=qt, pp=ppS: E_.matmul(
                        pp, KT[:, cs(sb, 128)], QT[:, cs(qt, 512)],
                        start=True, stop=False),
                        reads=[r_kT, r_qT], writes=[prS], tag="p2.st")
                    S.op("tensor", lambda E_, sb=sb, qt=qt, pp=ppS: E_.matmul(
                        pp, krT[:, cs(sb, 128)], QRT[:, cs(qt, 512)],
                        start=False, stop=True),
                        reads=[r_krT, r_qrT], par_writes=[prS], tag="p2.str")
                    if sb >= ext - 8:
                        rb = sb - (ext - 8)
                        S.op("vector", lambda E_, rb=rb, pp=ppS: E_.tensor_tensor(
                            pp, pp, MASK(rb), ALU.add),
                            reads=[prS, r_mask2], writes=[prS], tag="p2.msk")
                    S.op("scalar", lambda E_, slot=slot, pp=ppS: E_.activation(
                        PROBS(slot), pp, AF.Exp),
                        reads=[prS], writes=[r_probs[slot]], tag="p2.exp")
                    if sb == 1 and pending_tail[0] is not None:
                        pending_tail[0]()
                        pending_tail[0] = None
                    if sb >= 3:
                        av_rs(sb - 3)
                av_rs(ext - 3)
                av_rs(ext - 2)
                av_rs(ext - 1)
                def _rcp(E_):
                    with nc.allow_low_precision("softmax normalizer bf16"):
                        return E_.reciprocal(rsrow, pD[0:1, 0:512])
                S.op("vector", _rcp,
                    reads=[r_pD], writes=[r_rs], tag="p2.rcp")

                def tail(h=h, qt=qt):
                    S.op("tensor", lambda E_: E_.matmul(
                        POq[2], oner[:], rsrow, start=True, stop=True),
                        reads=[r_const, r_rs], writes=[r_pO[2]], tag="p2.bc")
                    S.op("scalar", lambda E_: E_.copy(bcast_sb[:], POq[2]),
                         reads=[r_pO[2]], writes=[r_bc], tag="p2.bce")
                    S.op("vector", lambda E_, h=h, qt=qt: E_.tensor_tensor(
                        big16[:, h * 2048 + qt * 512:h * 2048 + (qt + 1) * 512],
                        pC[:], bcast_sb[:], ALU.mult),
                        reads=[r_pC, r_bc], par_writes=[r_avt], tag="p2.avte")
                pending_tail[0] = tail

        if pending_tail[0] is not None:
            pending_tail[0]()
            pending_tail[0] = None

        if "avt" in tap_outs:
            tap_copy("avt", [(big16[:, cs(hh, 2048)], 128, 2048)
                             for hh in range(H)], [r_avt])

        r_wo = S.alias(r_mask2, r_kT, r_vT, r_v, r_qT, r_qrT, r_tabq,
                       *r_probs, *r_whead, name="woB")
        for db in range(DB):
            load(WO(db), wo_s[cs(db, 128), :], r_wo, rr=[R_cast["wo"]],
                 tag="p3.lwo")
        r_xn2 = S.alias(r_avt, name="xn2")

        r_wst = [S.alias(r_qlB, name="wst0"), S.alias(r_qlB, name="wst1")]
        WST = [fQ[:, 0:2048], fQ[:, 2048:4096]]
        pr_cyc = [r_pA, r_pB, r_pC, r_pD]
        pp_cyc = [pA, pB, pC, pD]
        wo_bank = [0]

        def emit_wo(tb):
            sl = tb % 2
            for dc in range(4):
                bk = wo_bank[0]
                wo_bank[0] = (bk + 1) % 4
                pp_wo, pr_wo = pp_cyc[bk], pr_cyc[bk]
                for hh in range(H):
                    S.op("tensor", lambda E_, tb=tb, dc=dc, hh=hh, pp=pp_wo:
                         E_.matmul(
                             pp[:],
                             big16[:, hh * 2048 + tb * 128:hh * 2048 + (tb + 1) * 128],
                             WO(hh)[:, cs(dc, 512)],
                             start=(hh == 0), stop=(hh == H - 1)),
                         reads=[r_avt, r_wo],
                         writes=[pr_wo] if hh == 0 else [],
                         par_writes=[] if hh == 0 else [pr_wo], tag="p3.wo")
                S.op("scalar", lambda E_, dc=dc, sl=sl, pp=pp_wo: E_.copy(
                    WST[sl][:, cs(dc, 512)], pp[:]),
                    reads=[pr_wo], par_writes=[r_wst[sl]], tag="p3.woe")

        emit_wo(0)
        for tb in range(TB):
            sl = tb % 2
            if tb + 1 < TB:
                emit_wo(tb + 1)
            load(fS[:, 0:2048], x_q[cs(tb, 128), :], r_stage[0], tag="p3.ldx",
                 par=False)
            ln_stats2([(WST[sl], 4)], [r_wst[sl]], sl, "p3.ln2")
            ln_apply2(fS[:, 2048:4096], WST[sl][:], [r_wst[sl]], r_stage[1],
                      sl, "p3.ln2")
            S.op("vector", lambda E_: E_.tensor_add(
                fS[:, 2048:4096], fS[:, 0:2048], fS[:, 2048:4096]),
                reads=[r_stage[0], r_stage[1]], writes=[r_stage[1]], tag="p3.x2")
            S.op("gpsimd", lambda E_, tb=tb: E_.dma_start(
                out=x2_s[cs(tb, 128), :], in_=fS[:, 2048:4096]),
                reads=[r_stage[1]], par_writes=[r_x2s], dma=True, tag="p3.spx2")
            ln_stats2([(fS[:, 2048:4096], 4)], [r_stage[1]], sl, "p3.ln3")
            ln_apply2(fS[:, 2048:4096], fS[:, 2048:4096], [r_stage[1]],
                      r_stage[1], sl, "p3.ln3")
            for db in range(DB):
                pp_t = [POq[0], POq[1]][db % 2]
                pr_t = [r_pO[0], r_pO[1]][db % 2]
                S.op("tensor", lambda E_, db=db, pp=pp_t: E_.transpose(
                    pp[:, 0:128], fS[:, 2048 + db * 128:2048 + (db + 1) * 128],
                    id_f[:]),
                    reads=[r_stage[1], r_const], writes=[pr_t], tag="p3.t")
                S.op("scalar", lambda E_, db=db, tb=tb, pp=pp_t: E_.copy(
                    big16[:, db * 2048 + tb * 128:db * 2048 + (tb + 1) * 128],
                    pp[:, 0:128]),
                    reads=[pr_t], par_writes=[r_xn2], tag="p3.te")
            lg_ps, lg_r = POq[2], r_pO[2]
            for db in range(DB):
                S.op("tensor", lambda E_, db=db, tb=tb, pp=lg_ps: E_.matmul(
                    pp[:, 0:E],
                    big16[:, db * 2048 + tb * 128:db * 2048 + (tb + 1) * 128],
                    gwT[:, cs(db, E)], start=(db == 0), stop=(db == DB - 1)),
                    reads=[r_xn2, r_gwT],
                    writes=[lg_r] if db == 0 else [],
                    par_writes=[] if db == 0 else [lg_r], tag="p3.lg")
            S.op("scalar", lambda E_, pp=lg_ps: E_.activation(
                fT[:, 64:80], pp[:, 0:E], AF.Sigmoid),
                reads=[lg_r], writes=[r_fT], tag="p3.sig")
            S.op("vector", lambda E_: E_.tensor_add(
                fT[:, 80:96], fT[:, 64:80], ebias_t[:]),
                reads=[r_fT, r_const], writes=[r_fT], tag="p3.rb")
            S.op("vector", lambda E_: E_.tensor_copy(fT[:, 96:112], fT[:, 80:96]),
                 reads=[r_fT], writes=[r_fT], tag="p3.rc")
            for k in range(3):
                S.op("vector", lambda E_: E_.reduce_max(
                    fT[:, 112:113], fT[:, 96:112], axis=AX.X),
                    reads=[r_fT], writes=[r_fT], tag="p3.rm")
                S.op("vector", lambda E_: E_.tensor_single_scalar(
                    fT[:, 128:144], fT[:, 96:112], fT[:, 112:113], ALU.is_equal),
                    reads=[r_fT], writes=[r_fT], tag="p3.req")
                S.op("vector", lambda E_: E_.scalar_tensor_tensor(
                    fT[:, 96:112], fT[:, 128:144], NEG, fT[:, 96:112],
                    ALU.mult, ALU.add),
                    reads=[r_fT], writes=[r_fT], tag="p3.rel")
            S.op("vector", lambda E_: E_.reduce_max(
                fT[:, 112:113], fT[:, 96:112], axis=AX.X),
                reads=[r_fT], writes=[r_fT], tag="p3.rm4")
            S.op("vector", lambda E_: E_.tensor_single_scalar(
                fT[:, 128:144], fT[:, 80:96], fT[:, 112:113], ALU.is_ge),
                reads=[r_fT], writes=[r_fT], tag="p3.rge")
            S.op("vector", lambda E_: E_.tensor_mul(
                fT[:, 96:112], fT[:, 64:80], fT[:, 128:144]),
                reads=[r_fT], writes=[r_fT], tag="p3.rw")
            S.op("vector", lambda E_: E_.reduce_sum(
                fT[:, 113:114], fT[:, 96:112], axis=AX.X),
                reads=[r_fT], writes=[r_fT], tag="p3.rs")
            S.op("vector", lambda E_: E_.reciprocal(fT[:, 114:115], fT[:, 113:114]),
                 reads=[r_fT], writes=[r_fT], tag="p3.rrc")
            S.op("vector", lambda E_, tb=tb: E_.tensor_scalar_mul(
                cw_all[:, cs(tb, E)], fT[:, 96:112], fT[:, 114:115]),
                reads=[r_fT], par_writes=[r_cw], tag="p3.cw")
        if "xn2T" in tap_outs:
            tap_copy("xn2T", [(big16[:, cs(db, 2048)], 128, 2048)
                              for db in range(DB)], [r_xn2])
        if "cw" in tap_outs:
            tap_copy("cw", [(cw_all[:], 128, TB * E)], [r_cw])

        r_wgup = [S.alias(r_wo, name=f"wgup{i}") for i in range(4)]
        r_hT2 = [S.alias(r_ckvA, name="hT0"), S.alias(r_ckvA, name="hT1")]
        r_silu = [S.alias(r_stage[0], name=f"silu{i}") for i in range(4)]
        SILU = lambda i: fSb[:, i * 512:(i + 1) * 512]
        HT = lambda ibg: ckvA[:, ibg * 2048:(ibg + 1) * 2048]

        for e in range(E + 1):
            for ih in range(2):
                k2 = (2 * e + ih) % 2
                gsl, usl = 2 * k2, 2 * k2 + 1
                for db in range(DB):
                    S.op("sync", lambda E_, e=e, ih=ih, db=db, gsl=gsl:
                         E_.dma_start_transpose(
                             wA[:, WGUP(gsl) + db * 512:WGUP(gsl) + (db + 1) * 512],
                             wg_s[e][cs(ih, 512), cs(db, 128)]),
                         reads=[R_cast[f"wg{e}"]], par_writes=[r_wgup[gsl]],
                         dma=True, tag="p4.lwg")
                    S.op("sync", lambda E_, e=e, ih=ih, db=db, usl=usl:
                         E_.dma_start_transpose(
                             wA[:, WGUP(usl) + db * 512:WGUP(usl) + (db + 1) * 512],
                             wu_s[e][cs(ih, 512), cs(db, 128)]),
                         reads=[R_cast[f"wu{e}"]], par_writes=[r_wgup[usl]],
                         dma=True, tag="p4.lwu")
                for tc in range(4):
                    for ib in range(4):
                        k = tc * 4 + ib
                        prg, ppg = pr_cyc[(2 * k) % 4], pp_cyc[(2 * k) % 4]
                        pru, ppu = pr_cyc[(2 * k + 1) % 4], pp_cyc[(2 * k + 1) % 4]
                        for db in range(DB):
                            S.op("tensor", lambda E_, gsl=gsl, db=db, ib=ib,
                                 tc=tc, pp=ppg: E_.matmul(
                                     pp[:],
                                     wA[:, WGUP(gsl) + db * 512 + ib * 128:
                                        WGUP(gsl) + db * 512 + (ib + 1) * 128],
                                     big16[:, db * 2048 + tc * 512:
                                           db * 2048 + (tc + 1) * 512],
                                     start=(db == 0), stop=(db == DB - 1)),
                                 reads=[r_wgup[gsl], r_xn2], writes=[prg],
                                 tag="p4.g")
                        for db in range(DB):
                            S.op("tensor", lambda E_, usl=usl, db=db, ib=ib,
                                 tc=tc, pp=ppu: E_.matmul(
                                     pp[:],
                                     wA[:, WGUP(usl) + db * 512 + ib * 128:
                                        WGUP(usl) + db * 512 + (ib + 1) * 128],
                                     big16[:, db * 2048 + tc * 512:
                                           db * 2048 + (tc + 1) * 512],
                                     start=(db == 0), stop=(db == DB - 1)),
                                 reads=[r_wgup[usl], r_xn2], writes=[pru],
                                 tag="p4.u")
                        ssl = k % 4
                        S.op("scalar", lambda E_, pp=ppg, ssl=ssl: E_.activation(
                            SILU(ssl), pp[:], AF.Silu),
                            reads=[prg], writes=[r_silu[ssl]], tag="p4.silu")
                        ibg = ih * 4 + ib
                        S.op("vector", lambda E_, ibg=ibg, tc=tc, ssl=ssl,
                             pp=ppu: E_.tensor_tensor(
                                 HT(ibg)[:, cs(tc, 512)], pp[:], SILU(ssl),
                                 ALU.mult),
                            reads=[pru, r_silu[ssl]],
                            par_writes=[r_hT2[ih]], tag="p4.h")
                S.op("gpsimd", lambda E_, e=e, ih=ih: E_.dma_start(
                    out=hT_s[e][cs(ih, 512), :].rearrange(
                        "(ib p) t -> p ib t", p=128),
                    in_=ckvA[:, ih * 8192:(ih + 1) * 8192].rearrange(
                        "p (ib t) -> p ib t", ib=4)),
                    reads=[r_hT2[ih]], par_writes=[r_hTs[e]], dma=True,
                    tag="p4.sp")

        if "hT2" in tap_outs:
            S.op("gpsimd", lambda E_: E_.dma_start(
                out=tap_outs["hT2"][:], in_=hT_s[2][:]),
                reads=[r_hTs[2]], par_writes=[r_out], dma=True, tag="tap.h2")
        if "hTs" in tap_outs:
            S.op("gpsimd", lambda E_: E_.dma_start(
                out=tap_outs["hTs"][:], in_=hT_s[E][:]),
                reads=[r_hTs[E]], par_writes=[r_out], dma=True, tag="tap.hs")

        r_wdt = [S.alias(r_qlB, *r_wst, name="wdtA"),
                 S.alias(*r_wgup, name="wdtB")]
        r_hst = [S.alias(*r_hT2, name="hstA"),
                 S.alias(*r_wgup, name="hstB")]
        r_acc = S.alias(r_avt, r_xn2, name="acc")
        WDTs = [qlB[:, 0:8192], WDT]
        HSTs = [ckvA[:, 0:16384], HSTB]

        for dh in range(2):
            for e in range(E + 1):
                sl = e % 2
                for ib in range(IB):
                    S.op("sync", lambda E_, e=e, dh=dh, ib=ib, sl=sl:
                         E_.dma_start_transpose(
                             WDTs[sl][:, cs(ib, 1024)],
                             wd_s[e][cs(dh, 1024), cs(ib, 128)]),
                         reads=[R_cast[f"wd{e}"]], par_writes=[r_wdt[sl]],
                         dma=True, tag="p5.lwd")
                S.op("scalar", lambda E_, e=e, sl=sl: E_.dma_start(
                    out=HSTs[sl][:].rearrange("p (ib t) -> p ib t", ib=8),
                    in_=hT_s[e][:].rearrange("(ib p) t -> p ib t", p=128)),
                    reads=[r_hTs[e]], writes=[r_hst[sl]], dma=True, tag="p5.lh")
                for tb in range(TB):
                    for dc in range(2):
                        k = tb * 2 + dc
                        pr, pp = pr_cyc[k % 4], pp_cyc[k % 4]
                        for ib in range(IB):
                            S.op("tensor", lambda E_, sl=sl, ib=ib, tb=tb,
                                 dc=dc, pp=pp: E_.matmul(
                                     pp[:],
                                     HSTs[sl][:, ib * 2048 + tb * 128:
                                              ib * 2048 + (tb + 1) * 128],
                                     WDTs[sl][:, ib * 1024 + dc * 512:
                                              ib * 1024 + (dc + 1) * 512],
                                     start=(ib == 0), stop=(ib == IB - 1)),
                                 reads=[r_hst[sl], r_wdt[sl]], writes=[pr],
                                 tag="p5.d")
                        acc_sl = accF[:, tb * 1024 + dc * 512:
                                      tb * 1024 + (dc + 1) * 512]
                        if e == 0:
                            S.op("vector", lambda E_, a=acc_sl, tb=tb, e=e,
                                 pp=pp: E_.tensor_single_scalar(
                                     a, pp[:], cw_all[:, tb * E + e:tb * E + e + 1],
                                     ALU.mult),
                                 reads=[pr, r_cw], par_writes=[r_acc],
                                 tag="p5.a0")
                        elif e < E:
                            S.op("vector", lambda E_, a=acc_sl, tb=tb, e=e,
                                 pp=pp: E_.scalar_tensor_tensor(
                                     a, pp[:], cw_all[:, tb * E + e:tb * E + e + 1],
                                     a, ALU.mult, ALU.add),
                                 reads=[pr, r_cw, r_acc], par_writes=[r_acc],
                                 tag="p5.ae")
                        else:
                            S.op("vector", lambda E_, a=acc_sl, pp=pp:
                                 E_.tensor_tensor(a, pp[:], a, ALU.add),
                                 reads=[pr, r_acc], par_writes=[r_acc],
                                 tag="p5.as")
            if dh == 0:
                S.op("gpsimd", lambda E_: E_.dma_start(
                    out=y0_s[:].rearrange("(tb p) d -> p tb d", p=128),
                    in_=accF[:].rearrange("p (tb d) -> p tb d", tb=16)),
                    reads=[r_acc], par_writes=[r_y0s], dma=True, tag="p5.spy")
                r_acc = S.alias(r_acc, r_y0s, name="acc1")

        if "y0" in tap_outs:
            S.op("sync", lambda E_: E_.dma_start(
                out=tap_outs["y0"][:], in_=y0_s[:]),
                reads=[r_y0s], par_writes=[r_out], dma=True, tag="tap.y0")
        if "acc" in tap_outs:
            S.op("sync", lambda E_: E_.dma_start(
                out=tap_outs["acc"][:], in_=accF[:]),
                reads=[r_acc], par_writes=[r_out], dma=True, tag="tap.acc")

        r_p6 = [S.alias(r_stage[0], r_stage[1], name="p6a"),
                S.alias(r_wdt[0], name="p6b")]
        P6B = [(fS[:, 0:1024], fS[:, 1024:2048], fS[:, 2048:4096],
                fS[:, 0:2048]),
               (fQ[:, 0:1024], fQ[:, 1024:2048], fQ[:, 2048:4096],
                fQ[:, 0:2048])]
        for tb in range(TB):
            sl = tb % 2
            ybuf, zbuf, xbuf, yzbuf = P6B[sl]
            rb6 = r_p6[sl]
            load(ybuf, y0_s[cs(tb, 128), :], rb6, tag="p6.ly", par=False)
            load(xbuf, x2_s[cs(tb, 128), :], rb6, q="scalar", tag="p6.lx2",
                 par=True)
            acc_t = accF[:, cs(tb, 1024)]
            ln_stats2([(ybuf, 2), (acc_t, 2)], [rb6, r_acc], sl, "p6.ln4")
            ln_apply2(ybuf, ybuf, [rb6], rb6, sl, "p6.af0", par=True)
            ln_apply2(zbuf, acc_t, [r_acc], rb6, sl, "p6.af1", par=True)
            S.op("vector", lambda E_, x=xbuf, yz=yzbuf: E_.tensor_add(
                x, x, yz),
                reads=[rb6], writes=[rb6], tag="p6.add")
            S.op("sync", lambda E_, tb=tb, x=xbuf: E_.dma_start(
                out=out[cs(tb, 128), :], in_=x),
                reads=[rb6], par_writes=[r_out], dma=True, tag="p6.out")

        S.finalize()
        with nc.Block() as block:
            S.emit(block, sems)
    return nc


def _rope_tables(rd, max_t, theta=10000.0):
    freqs = 1.0 / theta ** (np.arange(0, rd, 2, dtype=np.float32) / rd)
    t = np.arange(max_t, dtype=np.float32)
    ang = np.outer(t, freqs)
    cos = np.tile(np.cos(ang), (1, 2))
    sin = np.tile(np.sin(ang), (1, 2))
    return cos, sin


def make_core_inputs(full, c):
    b, p = c // 2, c % 2
    f32 = np.float32
    bf = ml_dtypes.bfloat16
    cos, sin = _rope_tables(RD, T)
    scale = f32(1.0) / np.sqrt(f32(HD + RD))

    def fold(s):
        o = s.T.copy()
        o[0:32] *= -1.0
        return o

    sl = np.arange(128)[:, None]
    tl = np.arange(512)[None, :]
    m = np.empty((128, 8 * 512), dtype=np.float32)
    for rb in range(8):
        m[:, rb * 512:(rb + 1) * 512] = np.where(
            (128 * rb + sl) <= (2 * tl + p), 0.0, NEG)
    return {
        "x_b": np.ascontiguousarray(full["L0_attn_x"][b]),
        "x_q": np.ascontiguousarray(full["L0_attn_x"][b, p::2]),
        "wqd": full["L0_attn_Wqd"], "wqu": full["L0_attn_Wqu"],
        "wqr": full["L0_attn_Wqr"], "wkvd": full["L0_attn_Wkvd"],
        "wku": full["L0_attn_Wku"], "wvu": full["L0_attn_Wvu"],
        "wkr": full["L0_attn_Wkr"], "wo": full["L0_attn_Wo"],
        "gate_w": full["L0_ffn_gate_w"],
        "ebias_bc": np.ascontiguousarray(
            np.tile(full["L0_ffn_expert_bias"][None, :], (128, 1))),
        "w_gate": full["L0_ffn_w_gate"], "w_up": full["L0_ffn_w_up"],
        "w_down": full["L0_ffn_w_down"],
        "sw_gate": full["L0_ffn_sw_gate"], "sw_up": full["L0_ffn_sw_up"],
        "sw_down": full["L0_ffn_sw_down"],
        "cosq": np.ascontiguousarray(cos.T[:, p::2] * scale).astype(bf),
        "sinq": np.ascontiguousarray(fold(sin)[:, p::2] * scale).astype(bf),
        "cosk": np.ascontiguousarray(cos.T).astype(bf),
        "sink": np.ascontiguousarray(fold(sin)).astype(bf),
        "maskin": m.astype(bf),
        "ident": np.eye(128, dtype=np.float32),
        "ones_col": np.ones((128, 1), dtype=bf),
        "ones_row": np.ones((1, 128), dtype=bf),
    }


def kernel(**inputs):
    nc = build_program()
    in_maps = [make_core_inputs(inputs, c) for c in range(8)]
    res = run_bass_kernel_spmd(nc, in_maps, core_ids=list(range(8)))
    out = np.empty((B, T, D), dtype=np.float32)
    for c in range(8):
        b, p = c // 2, c % 2
        out[b, p::2] = res.results[c]["out"]
    return out


# version 14
# speedup vs baseline: 1.1932x; 1.1932x over previous
import numpy as np
import ml_dtypes
import concourse.bass as bass
import concourse.mybir as mybir
from concourse.bass_utils import run_bass_kernel_spmd

F32 = mybir.dt.float32
BF16 = mybir.dt.bfloat16
AF = mybir.ActivationFunctionType
ALU = mybir.AluOpType
AX = mybir.AxisListType

B, T, D = 4, 4096, 2048
H, LAT, RD, HD = 16, 512, 64, 128
E, I = 16, 1024
TQ = T // 2
NEG = -1.0e30

DB = D // 128
SB = T // 128
TB = TQ // 128
LB = LAT // 128
IB = I // 128
NLANES = 8


def cs(i, w):
    return slice(i * w, (i + 1) * w)


class Op:
    __slots__ = ("eng", "fn", "deps", "dma", "lane", "need_inc", "idx", "tag",
                 "serial")

    def __init__(self, eng, fn, dma, lane, tag, serial):
        self.eng, self.fn, self.dma, self.lane = eng, fn, dma, lane
        self.deps = []
        self.need_inc = False
        self.idx = None
        self.tag = tag
        self.serial = serial


class Reg:
    __slots__ = ("writers", "readers", "name")

    def __init__(self, name=""):
        self.writers = {}
        self.readers = {}
        self.name = name


class Sched:
    COMPUTE = {"tensor": "t1", "vector": "v1", "scalar": "a1", "gpsimd": "g1",
               "sync": "s1"}

    def __init__(self):
        self.ops = []
        self.lane_ctr = {}

    def stream(self, o):
        if o.dma:
            return f"q_{o.eng}_{o.lane}"
        return self.COMPUTE[o.eng]

    def op(self, eng, fn, reads=(), writes=(), par_writes=(), dma=False, tag=""):
        lane = None
        if dma:
            c = self.lane_ctr.get(eng, 0)
            self.lane_ctr[eng] = c + 1
            lane = c % NLANES
        o = Op(eng, fn, dma, lane, tag, len(self.ops))
        st = self.stream(o)
        deps = {}

        def add(d):
            if d is not None and d is not o and (d.serial not in deps):
                deps[d.serial] = d

        for r in reads:
            for w in r.writers.values():
                add(w)
            r.readers[st] = o
        for r in writes:
            for w in r.writers.values():
                add(w)
            for rd in r.readers.values():
                add(rd)
            r.writers = {st: o}
            r.readers = {}
        for r in par_writes:
            for rd in r.readers.values():
                add(rd)
            r.writers[st] = o
        o.deps = [deps[k] for k in sorted(deps)]
        self.ops.append(o)
        return o

    @staticmethod
    def alias(*olds, name=""):
        r = Reg(name)
        for old in olds:
            for d in (old.writers, old.readers):
                for st, o in d.items():
                    cur = r.readers.get(st)
                    if cur is None or cur.serial < o.serial:
                        r.readers[st] = o
        return r

    def skip_edge(self, d, o):
        return (not d.dma) and d.eng == "tensor" and o.eng == "tensor"

    def finalize(self):
        for o in self.ops:
            for d in o.deps:
                if not self.skip_edge(d, o):
                    d.need_inc = True
        counts = {}
        for o in self.ops:
            if o.need_inc or o.dma:
                s = self.stream(o)
                counts[s] = counts.get(s, 0) + (16 if o.dma else 1)
                o.idx = counts[s]
        self.totals = counts

    def emit(self, block, sems):
        by_eng = {}
        for o in self.ops:
            by_eng.setdefault(o.eng, []).append(o)
        sched = self

        def run(eng_name):
            ops = by_eng.get(eng_name, [])

            def body(eng):
                hiwater = {}
                for o in ops:
                    for d in o.deps:
                        if sched.skip_edge(d, o):
                            continue
                        s = sched.stream(d)
                        if d.idx is None:
                            raise RuntimeError(f"no idx: {d.tag} -> {o.tag}")
                        if hiwater.get(s, 0) < d.idx:
                            hiwater[s] = d.idx
                            eng.wait_ge(sems[s], d.idx)
                    ins = o.fn(eng)
                    if o.idx is not None:
                        ins.then_inc(sems[sched.stream(o)], 16 if o.dma else 1)
                if eng_name == "sync":
                    for s, tot in sorted(sched.totals.items()):
                        if s.startswith("q_"):
                            eng.wait_ge(sems[s], tot)
            return body

        for name in ("sync", "scalar", "vector", "tensor", "gpsimd"):
            getattr(block, name)(run(name))


def build_program(taps=()):
    nc = bass.Bass("TRN2", target_bir_lowering=False)
    S = Sched()

    def din(name, shape, dt=F32):
        return nc.dram_tensor(name, shape, dt, kind="ExternalInput").ap()

    x_b = din("x_b", [T, D])
    x_q = din("x_q", [TQ, D])
    wqd = din("wqd", [D, LAT]);  wqu = din("wqu", [LAT, D])
    wqr = din("wqr", [LAT, H * RD]);  wkvd = din("wkvd", [D, LAT])
    wku = din("wku", [LAT, D]);  wvu = din("wvu", [LAT, D])
    wkr = din("wkr", [D, RD]);  wo = din("wo", [D, D])
    gate_w = din("gate_w", [E, D]);  ebias_bc = din("ebias_bc", [128, E])
    w_gate = din("w_gate", [E, I, D]);  w_up = din("w_up", [E, I, D])
    w_down = din("w_down", [E, D, I])
    sw_gate = din("sw_gate", [I, D]);  sw_up = din("sw_up", [I, D])
    sw_down = din("sw_down", [D, I])
    cosq = din("cosq", [RD, TQ], BF16);  sinq = din("sinq", [RD, TQ], BF16)
    cosk = din("cosk", [RD, T], BF16);  sink = din("sink", [RD, T], BF16)
    maskin = din("maskin", [128, 8 * 512], BF16)
    ident = din("ident", [128, 128])
    ones_col = din("ones_col", [128, 1], BF16)
    ones_row = din("ones_row", [1, 128], BF16)

    out = nc.dram_tensor("out", [TQ, D], F32, kind="ExternalOutput").ap()

    def scr(name, shape, dt=BF16):
        return nc.dram_tensor(name, shape, dt, kind="Internal").ap()

    wqd_s = scr("wqd_s", [D, LAT]);  wkvd_s = scr("wkvd_s", [D, LAT])
    wkr_s = scr("wkr_s", [D, RD]);  wqu_s = scr("wqu_s", [LAT, D])
    wqr_s = scr("wqr_s", [LAT, H * RD]);  wku_s = scr("wku_s", [LAT, D])
    wvu_s = scr("wvu_s", [LAT, D]);  wo_s = scr("wo_s", [D, D])
    gw_s = scr("gw_s", [E, D])
    wg_s = scr("wg_s", [E + 1, I, D]);  wu_s = scr("wu_s", [E + 1, I, D])
    wd_s = scr("wd_s", [E + 1, D, I])
    hT_s = scr("hT_s", [E + 1, I, TQ])
    x2_s = scr("x2_s", [TQ, D], F32)
    y0_s = scr("y0_s", [TQ, D // 2], F32)

    tap_outs = {}
    for name, shape in taps:
        tap_outs[name] = nc.dram_tensor("tap_" + name, shape, F32,
                                        kind="ExternalOutput").ap()

    from contextlib import ExitStack
    with ExitStack() as ctx:
        sbuf = lambda n, sh, dt: ctx.enter_context(nc.sbuf_tensor(n, sh, dt))
        psum = lambda n, sh, dt=F32: ctx.enter_context(nc.psum_tensor(n, sh, dt))

        big16 = sbuf("big16", [128, 16 * 2048], BF16)
        ckvA = sbuf("ckvA", [128, 8 * 2048], BF16)
        qlB = sbuf("qlB", [128, 4 * 2048], BF16)
        wA = sbuf("wA", [128, 16 * 2048], BF16)
        fS = sbuf("fS", [128, 2 * 2048], F32)
        ropetmp = sbuf("ropetmp", [64, 512], F32)
        rsrow = None
        fT = sbuf("fT", [128, 160], F32)
        id_f = sbuf("id_f", [128, 128], F32)
        onec = sbuf("onec", [128, 1], BF16)
        oner = sbuf("oner", [1, 128], BF16)
        gwT = sbuf("gwT", [128, DB * E], BF16)
        ebias_t = sbuf("ebias_t", [128, E], F32)
        krT = sbuf("krT", [RD, T], BF16)
        bcast_sb = sbuf("bcast_sb", [128, 512], F32)
        cw_all = sbuf("cw_all", [128, TB * E], F32)

        pX = psum("pX", [128, 2048])
        pA = pX[:, 0:512]
        pB = pX[:, 512:1024]
        pC = pX[:, 1024:1536]
        pD = pX[:, 1536:2048]
        pO = psum("pO", [128, 2048])

        sems = {}
        for s in ("t1", "v1", "a1", "g1", "s1"):
            sems[s] = ctx.enter_context(nc.semaphore(s))
        for q in ("sync", "gpsimd", "scalar"):
            for ln in range(NLANES):
                key = f"q_{q}_{ln}"
                sems[key] = ctx.enter_context(nc.semaphore(key))

        rsrow = ropetmp[0:1, :].bitcast(BF16)[0:1, 0:512]
        accF = big16[:].bitcast(F32)
        fQ = qlB[:].bitcast(F32)
        VTf = wA[:].bitcast(F32)
        fSb = fS[:].bitcast(BF16)

        WQD = lambda db: wA[:, db * 512:(db + 1) * 512]
        WKVD = lambda db: wA[:, 8192 + db * 512:8192 + (db + 1) * 512]
        WKR = lambda db: wA[:, 16384 + db * 64:16384 + (db + 1) * 64]
        XNBT = lambda db: wA[:, 17408 + db * 512:17408 + (db + 1) * 512]
        COSK = wA[0:64, 25600:29696]
        SINK = wA[64:128, 25600:29696]
        MASK = lambda rb: wA[:, rb * 512:(rb + 1) * 512]
        KT = wA[:, 4096:8192]
        VT32 = VTf[:, 4096:8192]
        VS = wA[:, 16384:20480]
        QT = wA[:, 20480:22528]
        QRT = wA[0:64, 22528:24576]
        COSQ = wA[0:64, 24576:26624]
        SINQ = wA[64:128, 24576:26624]
        PROBS = lambda i: wA[:, 26624 + i * 512:26624 + (i + 1) * 512]
        WHEAD = lambda sl: 28672 + sl * 1792
        WO = lambda db: wA[:, db * 2048:(db + 1) * 2048]
        WGUP = lambda sl: 8192 * sl
        WDT = wA[:, 0:8192]
        HSTB = wA[:, 8192:24576]

        r_const = Reg("const")
        r_const_eps = Reg("eps")
        r_fT = Reg("fT")
        r_fTa = [Reg("fTa0"), Reg("fTa1")]
        r_big = Reg("big16")
        r_ckvA = Reg("ckvA")
        r_qlB = Reg("qlB")
        r_krT = Reg("krT")
        r_gwT = Reg("gwT")
        r_bc = Reg("bcast")
        r_cw = Reg("cw")
        r_stage = [Reg(f"fS{i}") for i in range(2)]
        r_rope = Reg("ropetmp")
        r_rs = r_rope
        r_pA, r_pB, r_pC, r_pD = Reg("pA"), Reg("pB"), Reg("pC"), Reg("pD")
        r_pO = [Reg(f"pO{i}") for i in range(4)]
        r_wqd, r_wkvd, r_wkr = Reg("wqd"), Reg("wkvd"), Reg("wkr")
        r_xnbT, r_tabk = Reg("xnbT"), Reg("tabk")
        R_cast = {}
        r_hTs = [Reg(f"hTs{e}") for e in range(E + 1)]
        r_x2s = Reg("x2s")
        r_y0s = Reg("y0s")
        r_out = Reg("out")

        POq = [pO[:, cs(i, 512)] for i in range(4)]

        S.op("vector", lambda E_: E_.memset(fT[:, 158:159], 1e-5),
             writes=[r_const_eps], tag="p0.eps")

        def load(dst, src, rw, rr=(), q="sync", tag="ld", par=True):
            kw = dict(reads=list(rr), dma=True, tag=tag)
            if par:
                kw["par_writes"] = [rw]
            else:
                kw["writes"] = [rw]
            return S.op(q, lambda E_, d=dst, s=src: E_.dma_start(out=d, in_=s),
                        **kw)

        load(id_f[:], ident[:], r_const)
        load(onec[:], ones_col[:], r_const)
        load(oner[:], ones_row[:], r_const)
        load(ebias_t[:], ebias_bc[:], r_const)
        load(COSK, cosk[:], r_tabk)
        load(SINK, sink[:], r_tabk)

        def cast(dst, src, name, tag, rr=()):
            r = Reg("c_" + name)
            R_cast[name] = r
            S.op("gpsimd", lambda E_, d=dst, s=src: E_.dma_start(out=d, in_=s),
                 reads=list(rr), par_writes=[r], dma=True, tag=tag)

        cast(wqd_s[:], wqd[:], "wqd", "c.wqd")
        cast(wkvd_s[:], wkvd[:], "wkvd", "c.wkvd")
        cast(wkr_s[:], wkr[:], "wkr", "c.wkr")
        cast(wqu_s[:], wqu[:], "wqu", "c.wqu")
        cast(wqr_s[:], wqr[:], "wqr", "c.wqr")
        cast(wku_s[:], wku[:], "wku", "c.wku")
        cast(wvu_s[:], wvu[:], "wvu", "c.wvu")
        cast(gw_s[:], gate_w[:], "gw", "c.gw")


        def ln_bst(srcs, rr, area, tag):
            a = 29 * area
            ra = r_fTa[area]
            n = 0
            for ap, nch in srcs:
                for i in range(nch):
                    S.op("vector", lambda E_, ap=ap, i=i, n=n, a=a: E_.bn_stats(
                        fT[:, a + 6 * (n + i):a + 6 * (n + i) + 6],
                        ap[:, cs(i, 512)]),
                        reads=list(rr), par_writes=[ra], tag=tag + ".bst")
                n += nch
            return n

        def ln_fin(n, area, tag):
            a = 29 * area
            ra = r_fTa[area]
            S.op("vector", lambda E_, n=n, a=a: E_.bn_aggr(
                fT[:, a + 24:a + 26], fT[:, a:a + 6 * n]),
                reads=[ra], writes=[ra], tag=tag + ".agg")
            S.op("scalar", lambda E_, a=a: E_.activation(
                fT[:, a + 27:a + 28], fT[:, a + 25:a + 26], AF.Sqrt,
                bias=fT[:, 158:159]),
                reads=[ra, r_const_eps], writes=[ra], tag=tag + ".sqrt")
            S.op("vector", lambda E_, a=a: E_.reciprocal(
                fT[:, a + 28:a + 29], fT[:, a + 27:a + 28]),
                reads=[ra], writes=[ra], tag=tag + ".rcp")

        def ln_stats2(srcs, rr, area, tag):
            ln_fin(ln_bst(srcs, rr, area, tag), area, tag)

        def ln_apply2(dst, src, r_src, r_dst, area, tag, par=False):
            a = 29 * area
            kw = dict(writes=[r_dst]) if not par else dict(par_writes=[r_dst])
            S.op("vector", lambda E_, d=dst, s=src, a=a: E_.tensor_scalar(
                d, s, fT[:, a + 24:a + 25], fT[:, a + 28:a + 29],
                ALU.subtract, ALU.mult),
                reads=list(r_src) + [r_fTa[area]], **kw, tag=tag + ".aff")

        r_stq = [Reg("fQ0"), Reg("fQ1")]
        stage4 = [fS[:, 0:2048], fS[:, 2048:4096], fQ[:, 0:2048], fQ[:, 2048:4096]]
        r_stage4 = [r_stage[0], r_stage[1], r_stq[0], r_stq[1]]
        def p1a_tail(tb):
            st = r_stage4[tb % 4]
            stage = stage4[tb % 4]
            ln_fin(4, tb % 2, "p1.ln")
            ln_apply2(stage, stage, [st], st, tb % 2, "p1.ln")
            for db in range(DB):
                pr = [r_pA, r_pB][db % 2]
                pp = [pA, pB][db % 2]
                S.op("tensor", lambda E_, s=stage[:, cs(db, 128)], pp=pp:
                     E_.transpose(pp[:, 0:128], s, id_f[:]),
                    reads=[st, r_const], writes=[pr], tag="p1.tq")
                S.op("scalar", lambda E_, db=db, tb=tb, pp=pp: E_.copy(
                    big16[:, db * 2048 + tb * 128:db * 2048 + (tb + 1) * 128],
                    pp[:, 0:128]),
                    reads=[pr], par_writes=[r_big], tag="p1.tqe")

        for tb in range(TB):
            st = r_stage4[tb % 4]
            stage = stage4[tb % 4]
            load(stage, x_q[cs(tb, 128), :], st, q="scalar", tag="p1.ldxq",
                 par=False)
            ln_bst([(stage, 4)], [st], tb % 2, "p1.ln")
            if tb > 0:
                p1a_tail(tb - 1)
        p1a_tail(TB - 1)

        for db in range(DB):
            load(WQD(db), wqd_s[cs(db, 128), :], r_wqd, rr=[R_cast["wqd"]],
                 tag="ld.wqd")
            load(WKVD(db), wkvd_s[cs(db, 128), :], r_wkvd, rr=[R_cast["wkvd"]],
                 tag="ld.wkvd")
            load(WKR(db), wkr_s[cs(db, 128), :], r_wkr, rr=[R_cast["wkr"]],
                 tag="ld.wkr")

        CKVT = lambda lb: ckvA[:, lb * 4096:(lb + 1) * 4096]
        p1b_pending = [None]
        for sc in range(8):
            def p1b_tail(tile):
                st = r_stage4[tile % 4]
                stage = stage4[tile % 4]
                j = tile % 4
                ln_fin(4, tile % 2, "p1b.ln")
                ln_apply2(stage, stage, [st], st, tile % 2, "p1b.ln")
                for db in range(DB):
                    pr = [r_pA, r_pB][db % 2]
                    pp = [pA, pB][db % 2]
                    S.op("tensor", lambda E_, s=stage[:, cs(db, 128)], pp=pp:
                         E_.transpose(pp[:, 0:128], s, id_f[:]),
                        reads=[st, r_const], writes=[pr], tag="p1b.t")
                    S.op("scalar", lambda E_, db=db, j=j, pp=pp: E_.copy(
                        XNBT(db)[:, cs(j, 128)], pp[:, 0:128]),
                        reads=[pr], par_writes=[r_xnbT], tag="p1b.te")

            for j in range(4):
                tile = sc * 4 + j
                st = r_stage4[tile % 4]
                stage = stage4[tile % 4]
                load(stage, x_b[cs(tile, 128), :], st, tag="p1.ldxb", par=False)
                ln_bst([(stage, 4)], [st], tile % 2, "p1b.ln")
                if p1b_pending[0] is not None:
                    p1b_tail(p1b_pending[0])
                p1b_pending[0] = tile
            p1b_tail(p1b_pending[0])
            p1b_pending[0] = None
            for lb in range(LB):
                pr = [r_pC, r_pD][lb % 2]
                pp = [pC, pD][lb % 2]
                for db in range(DB):
                    S.op("tensor", lambda E_, lb=lb, db=db, pp=pp: E_.matmul(
                        pp[:], WKVD(db)[:, cs(lb, 128)], XNBT(db),
                        start=(db == 0), stop=(db == DB - 1)),
                        reads=[r_wkvd, r_xnbT], writes=[pr], tag="p1.ckv")
                S.op("scalar", lambda E_, lb=lb, sc=sc, pp=pp: E_.copy(
                    CKVT(lb)[:, cs(sc, 512)], pp[:]),
                    reads=[pr], par_writes=[r_ckvA], tag="p1.ckve")
            for db in range(DB):
                S.op("tensor", lambda E_, db=db: E_.matmul(
                    pO[0:64, 0:512], WKR(db), XNBT(db),
                    start=(db == 0), stop=(db == DB - 1)),
                    reads=[r_wkr, r_xnbT], writes=[r_pO[0]], tag="p1.kr")
            S.op("vector", lambda E_, sc=sc: E_.tensor_tensor(
                bcast_sb[0:64, :], pO[0:64, 0:512], COSK[:, cs(sc, 512)],
                ALU.mult), reads=[r_pO[0], r_tabk], writes=[r_bc], tag="p1.kr1")
            S.op("vector", lambda E_, sc=sc: E_.tensor_tensor(
                ropetmp[0:32, :], pO[32:64, 0:512],
                SINK[0:32, cs(sc, 512)], ALU.mult),
                reads=[r_pO[0], r_tabk], writes=[r_rope], tag="p1.kr2")
            S.op("vector", lambda E_, sc=sc: E_.tensor_tensor(
                ropetmp[32:64, :], pO[0:32, 0:512],
                SINK[32:64, cs(sc, 512)], ALU.mult),
                reads=[r_pO[0], r_tabk], par_writes=[r_rope], tag="p1.kr3")
            S.op("vector", lambda E_, sc=sc: E_.tensor_tensor(
                krT[:, cs(sc, 512)], bcast_sb[0:64, :], ropetmp[:],
                ALU.add),
                reads=[r_bc, r_rope], par_writes=[r_krT], tag="p1.kr4")

        for db in range(DB):
            S.op("sync", lambda E_, db=db: E_.dma_start_transpose(
                gwT[:, cs(db, E)], gw_s[:, cs(db, 128)]),
                reads=[R_cast["gw"]], par_writes=[r_gwT], dma=True, tag="ld.gwT")

        r_qlB = S.alias(r_qlB, r_stq[0], r_stq[1], name="qlB2")
        QLT = lambda lb: qlB[:, lb * 2048:(lb + 1) * 2048]
        for lb in range(LB):
            for tc in range(4):
                k = lb * 4 + tc
                pr = [r_pC, r_pD][k % 2]
                pp = [pC, pD][k % 2]
                for db in range(DB):
                    S.op("tensor", lambda E_, lb=lb, tc=tc, db=db, pp=pp: E_.matmul(
                        pp[:], WQD(db)[:, cs(lb, 128)],
                        big16[:, db * 2048 + tc * 512:db * 2048 + (tc + 1) * 512],
                        start=(db == 0), stop=(db == DB - 1)),
                        reads=[r_wqd, r_big], writes=[pr], tag="p1.ql")
                S.op("scalar", lambda E_, lb=lb, tc=tc, pp=pp: E_.copy(
                    QLT(lb)[:, cs(tc, 512)], pp[:]),
                    reads=[pr], par_writes=[r_qlB], tag="p1.qle")

        def tap_copy(name, src_aps, rr):
            if name not in tap_outs:
                return
            o = tap_outs[name]
            row = 0
            for ap, rows, width in src_aps:
                S.op("vector", lambda E_, ap=ap, rows=rows, width=width:
                     E_.tensor_copy(fS[0:rows, 0:width], ap),
                     reads=list(rr), writes=[r_stage[0], r_stage[1]], tag="tapc")
                S.op("sync", lambda E_, row=row, rows=rows, width=width: E_.dma_start(
                    out=o[row:row + rows, :], in_=fS[0:rows, 0:width]),
                    reads=[r_stage[0]], par_writes=[r_out], dma=True, tag="tapd")
                row += rows

        if "ckvT" in tap_outs:
            tap_copy("ckvT", [(CKVT(lb)[:, 0:4096], 128, 4096) for lb in range(4)],
                     [r_ckvA])
        if "krT" in tap_outs:
            tap_copy("krT", [(krT[:, :], 64, 4096)], [r_krT])
        if "qlT" in tap_outs:
            tap_copy("qlT", [(QLT(lb)[:, :], 128, 2048) for lb in range(4)],
                     [r_qlB])
        if "xnTq" in tap_outs:
            tap_copy("xnTq", [(big16[:, cs(db, 2048)], 128, 2048)
                              for db in range(DB)], [r_big])

        r_mask2 = S.alias(r_wqd, name="mask2")
        r_kT = S.alias(r_wqd, name="kT")
        r_vT = S.alias(r_wkvd, name="vT")
        r_v = S.alias(r_wkr, r_xnbT, name="v")
        r_qT = S.alias(r_xnbT, name="qT")
        r_qrT = S.alias(r_xnbT, name="qrT")
        r_tabq = S.alias(r_xnbT, r_tabk, name="tabq")
        r_probs = [S.alias(r_xnbT, r_tabk, name=f"probs{i}") for i in range(4)]
        r_whead = [S.alias(r_tabk, name=f"whead{i}") for i in range(2)]
        r_avt = S.alias(r_big, name="avt")

        for rb in range(8):
            load(MASK(rb), maskin[:, cs(rb, 512)], r_mask2, tag="ld.mask")
        load(COSQ, cosq[:], r_tabq, tag="ld.cosq")
        load(SINQ, sinq[:], r_tabq, tag="ld.sinq")

        pending_tail = [None]
        for h in range(H):
            if pending_tail[0] is not None:
                pending_tail[0]()
                pending_tail[0] = None
            sl = h % 2
            base = WHEAD(sl)
            rw = r_whead[sl]
            S.op("sync", lambda E_, h=h, base=base: E_.dma_start(
                out=wA[:, base:base + 512].rearrange("p (lb c) -> p lb c", lb=4),
                in_=wqu_s[:, cs(h, 128)].rearrange("(lb p) c -> p lb c", p=128)),
                reads=[R_cast["wqu"]], writes=[rw], dma=True, tag="p2.lwqu")
            S.op("sync", lambda E_, h=h, base=base: E_.dma_start(
                out=wA[:, base + 512:base + 1024].rearrange("p (lb c) -> p lb c", lb=4),
                in_=wku_s[:, cs(h, 128)].rearrange("(lb p) c -> p lb c", p=128)),
                reads=[R_cast["wku"]], par_writes=[rw], dma=True, tag="p2.lwku")
            S.op("scalar", lambda E_, h=h, base=base: E_.dma_start(
                out=wA[:, base + 1024:base + 1536].rearrange("p (lb c) -> p lb c", lb=4),
                in_=wvu_s[:, cs(h, 128)].rearrange("(lb p) c -> p lb c", p=128)),
                reads=[R_cast["wvu"]], par_writes=[rw], dma=True, tag="p2.lwvu")
            S.op("scalar", lambda E_, h=h, base=base: E_.dma_start(
                out=wA[:, base + 1536:base + 1792].rearrange("p (lb c) -> p lb c", lb=4),
                in_=wqr_s[:, cs(h, 64)].rearrange("(lb p) c -> p lb c", p=128)),
                reads=[R_cast["wqr"]], par_writes=[rw], dma=True, tag="p2.lwqr")

            cast(wg_s[h], w_gate[h] if h < E else sw_gate[:], f"wg{h}",
                 f"c.wg{h}", rr=[rw])
            cast(wu_s[h], w_up[h] if h < E else sw_up[:], f"wu{h}",
                 f"c.wu{h}", rr=[rw])
            cast(wd_s[h], w_down[h] if h < E else sw_down[:], f"wd{h}",
                 f"c.wd{h}", rr=[rw])
            if h == 8:
                cast(wo_s[:], wo[:], "wo", "c.wo", rr=[rw])
            if h == 15:
                cast(wg_s[E], sw_gate[:], f"wg{E}", "c.swg", rr=[rw])
                cast(wu_s[E], sw_up[:], f"wu{E}", "c.swu", rr=[rw])
                cast(wd_s[E], sw_down[:], f"wd{E}", "c.swd", rr=[rw])

            WQU_h = lambda lb: wA[:, base + lb * 128:base + (lb + 1) * 128]
            WKU_h = lambda lb: wA[:, base + 512 + lb * 128:base + 512 + (lb + 1) * 128]
            WVU_h = lambda lb: wA[:, base + 1024 + lb * 128:base + 1024 + (lb + 1) * 128]
            WQR_h = lambda lb: wA[:, base + 1536 + lb * 64:base + 1536 + (lb + 1) * 64]

            for tc in range(4):
                pr = [r_pC, r_pD][tc % 2]
                pp = [pC, pD][tc % 2]
                for lb in range(LB):
                    S.op("tensor", lambda E_, w=WQR_h(lb), q=QLT(lb)[:, cs(tc, 512)],
                         lb=lb, pp=pp: E_.matmul(
                        pp[0:64, :], w, q, start=(lb == 0), stop=(lb == LB - 1)),
                        reads=[rw, r_qlB], writes=[pr], tag="p2.qr")
                S.op("vector", lambda E_, tc=tc, pp=pp: E_.tensor_tensor(
                    bcast_sb[0:64, :], pp[0:64, :], COSQ[:, cs(tc, 512)],
                    ALU.mult), reads=[pr, r_tabq], writes=[r_bc], tag="p2.qr1")
                S.op("vector", lambda E_, tc=tc, pp=pp: E_.tensor_tensor(
                    ropetmp[0:32, :], pp[32:64, :], SINQ[0:32, cs(tc, 512)],
                    ALU.mult), reads=[pr, r_tabq], writes=[r_rope],
                    tag="p2.qr2")
                S.op("vector", lambda E_, tc=tc, pp=pp: E_.tensor_tensor(
                    ropetmp[32:64, :], pp[0:32, :], SINQ[32:64, cs(tc, 512)],
                    ALU.mult), reads=[pr, r_tabq], par_writes=[r_rope],
                    tag="p2.qr3")
                S.op("vector", lambda E_, tc=tc: E_.tensor_tensor(
                    QRT[:, cs(tc, 512)], bcast_sb[0:64, :], ropetmp[:],
                    ALU.add), reads=[r_bc, r_rope], par_writes=[r_qrT],
                    tag="p2.qr4")

            for scc in range(8):
                pr = [r_pA, r_pB][scc % 2]
                pp = [pA, pB][scc % 2]
                for lb in range(LB):
                    S.op("tensor", lambda E_, w=WKU_h(lb), c=CKVT(lb)[:, cs(scc, 512)],
                         lb=lb, pp=pp: E_.matmul(
                        pp[:], w, c, start=(lb == 0), stop=(lb == LB - 1)),
                        reads=[rw, r_ckvA], writes=[pr], tag="p2.kt")
                S.op("scalar", lambda E_, scc=scc, pp=pp: E_.copy(
                    KT[:, cs(scc, 512)], pp[:]),
                    reads=[pr], par_writes=[r_kT], tag="p2.kte")
            for scc in range(8):
                pr = [r_pA, r_pB][scc % 2]
                pp = [pA, pB][scc % 2]
                for lb in range(LB):
                    S.op("tensor", lambda E_, w=WVU_h(lb), c=CKVT(lb)[:, cs(scc, 512)],
                         lb=lb, pp=pp: E_.matmul(
                        pp[:], w, c, start=(lb == 0), stop=(lb == LB - 1)),
                        reads=[rw, r_ckvA], writes=[pr], tag="p2.vt")
                S.op("scalar", lambda E_, scc=scc, pp=pp: E_.copy(
                    VT32[:, cs(scc, 512)], pp[:]),
                    reads=[pr], par_writes=[r_vT], tag="p2.vte")
            for sb in range(SB):
                pr = r_pO[sb % 2]
                pp = POq[sb % 2]
                S.op("tensor", lambda E_, sb=sb, pp=pp: E_.transpose(
                    pp[:, 0:128], VT32[:, cs(sb, 128)], id_f[:]),
                    reads=[r_vT, r_const], writes=[pr], tag="p2.vtr")
                S.op("scalar", lambda E_, sb=sb, pp=pp: E_.copy(
                    VS[:, cs(sb, 128)], pp[:, 0:128]),
                    reads=[pr], par_writes=[r_v], tag="p2.vtre")
            scale = float(np.float32(1.0) / np.sqrt(np.float32(HD + RD)))
            for tc in range(4):
                pr = [r_pA, r_pB][tc % 2]
                pp = [pA, pB][tc % 2]
                for lb in range(LB):
                    S.op("tensor", lambda E_, w=WQU_h(lb), q=QLT(lb)[:, cs(tc, 512)],
                         lb=lb, pp=pp: E_.matmul(
                        pp[:], w, q, start=(lb == 0), stop=(lb == LB - 1)),
                        reads=[rw, r_qlB], writes=[pr], tag="p2.qt")
                S.op("scalar", lambda E_, tc=tc, pp=pp, scale=scale: E_.activation(
                    QT[:, cs(tc, 512)], pp[:], AF.Copy, scale=scale),
                    reads=[pr], par_writes=[r_qT], tag="p2.qte")
            for qt in range(4):
                ext = 8 * (qt + 1)
                st_regs = [r_pA, r_pB, r_pO[0], r_pO[1]]
                st_ps = [pA[:], pB[:], POq[0], POq[1]]

                def av_rs(sb, ext=ext):
                    slot = sb % 4
                    S.op("tensor", lambda E_, sb=sb, slot=slot: E_.matmul(
                        pC[:], VS[:, cs(sb, 128)], PROBS(slot),
                        start=(sb == 0), stop=(sb == ext - 1)),
                        reads=[r_v, r_probs[slot]],
                        writes=[r_pC] if sb == 0 else [],
                        par_writes=[] if sb == 0 else [r_pC], tag="p2.av")
                    S.op("tensor", lambda E_, sb=sb, slot=slot: E_.matmul(
                        pD[0:1, :], onec[:], PROBS(slot),
                        start=(sb == 0), stop=(sb == ext - 1)),
                        reads=[r_const, r_probs[slot]],
                        writes=[r_pD] if sb == 0 else [],
                        par_writes=[] if sb == 0 else [r_pD], tag="p2.rs")

                for sb in range(ext):
                    slot = sb % 4
                    prS, ppS = st_regs[slot], st_ps[slot]
                    S.op("tensor", lambda E_, sb=sb, qt=qt, pp=ppS: E_.matmul(
                        pp, KT[:, cs(sb, 128)], QT[:, cs(qt, 512)],
                        start=True, stop=False),
                        reads=[r_kT, r_qT], writes=[prS], tag="p2.st")
                    S.op("tensor", lambda E_, sb=sb, qt=qt, pp=ppS: E_.matmul(
                        pp, krT[:, cs(sb, 128)], QRT[:, cs(qt, 512)],
                        start=False, stop=True),
                        reads=[r_krT, r_qrT], par_writes=[prS], tag="p2.str")
                    if sb >= ext - 8:
                        rb = sb - (ext - 8)
                        S.op("vector", lambda E_, rb=rb, pp=ppS: E_.tensor_tensor(
                            pp, pp, MASK(rb), ALU.add),
                            reads=[prS, r_mask2], writes=[prS], tag="p2.msk")
                    S.op("scalar", lambda E_, slot=slot, pp=ppS: E_.activation(
                        PROBS(slot), pp, AF.Exp),
                        reads=[prS], writes=[r_probs[slot]], tag="p2.exp")
                    if sb == 1 and pending_tail[0] is not None:
                        pending_tail[0]()
                        pending_tail[0] = None
                    if sb >= 3:
                        av_rs(sb - 3)
                av_rs(ext - 3)
                av_rs(ext - 2)
                av_rs(ext - 1)
                def _rcp(E_):
                    with nc.allow_low_precision("softmax normalizer bf16"):
                        return E_.reciprocal(rsrow, pD[0:1, 0:512])
                S.op("vector", _rcp,
                    reads=[r_pD], writes=[r_rs], tag="p2.rcp")

                def tail(h=h, qt=qt):
                    S.op("tensor", lambda E_: E_.matmul(
                        POq[2], oner[:], rsrow, start=True, stop=True),
                        reads=[r_const, r_rs], writes=[r_pO[2]], tag="p2.bc")
                    S.op("scalar", lambda E_: E_.copy(bcast_sb[:], POq[2]),
                         reads=[r_pO[2]], writes=[r_bc], tag="p2.bce")
                    S.op("vector", lambda E_, h=h, qt=qt: E_.tensor_tensor(
                        big16[:, h * 2048 + qt * 512:h * 2048 + (qt + 1) * 512],
                        pC[:], bcast_sb[:], ALU.mult),
                        reads=[r_pC, r_bc], par_writes=[r_avt], tag="p2.avte")
                pending_tail[0] = tail

        if pending_tail[0] is not None:
            pending_tail[0]()
            pending_tail[0] = None

        if "avt" in tap_outs:
            tap_copy("avt", [(big16[:, cs(hh, 2048)], 128, 2048)
                             for hh in range(H)], [r_avt])

        r_wo = S.alias(r_mask2, r_kT, r_vT, r_v, r_qT, r_qrT, r_tabq,
                       *r_probs, *r_whead, name="woB")
        for db in range(DB):
            load(WO(db), wo_s[cs(db, 128), :], r_wo, rr=[R_cast["wo"]],
                 tag="p3.lwo")
        r_xn2 = S.alias(r_avt, name="xn2")

        r_wst = [S.alias(r_qlB, name="wst0"), S.alias(r_qlB, name="wst1")]
        WST = [fQ[:, 0:2048], fQ[:, 2048:4096]]
        pr_cyc = [r_pA, r_pB, r_pC, r_pD]
        pp_cyc = [pA, pB, pC, pD]
        wo_bank = [0]

        def emit_wo(tb):
            sl = tb % 2
            for dc in range(4):
                bk = wo_bank[0]
                wo_bank[0] = (bk + 1) % 4
                pp_wo, pr_wo = pp_cyc[bk], pr_cyc[bk]
                for hh in range(H):
                    S.op("tensor", lambda E_, tb=tb, dc=dc, hh=hh, pp=pp_wo:
                         E_.matmul(
                             pp[:],
                             big16[:, hh * 2048 + tb * 128:hh * 2048 + (tb + 1) * 128],
                             WO(hh)[:, cs(dc, 512)],
                             start=(hh == 0), stop=(hh == H - 1)),
                         reads=[r_avt, r_wo],
                         writes=[pr_wo] if hh == 0 else [],
                         par_writes=[] if hh == 0 else [pr_wo], tag="p3.wo")
                S.op("scalar", lambda E_, dc=dc, sl=sl, pp=pp_wo: E_.copy(
                    WST[sl][:, cs(dc, 512)], pp[:]),
                    reads=[pr_wo], par_writes=[r_wst[sl]], tag="p3.woe")

        emit_wo(0)
        for tb in range(TB):
            sl = tb % 2
            if tb + 1 < TB:
                emit_wo(tb + 1)
            load(fS[:, 0:2048], x_q[cs(tb, 128), :], r_stage[0], tag="p3.ldx",
                 par=False)
            ln_stats2([(WST[sl], 4)], [r_wst[sl]], sl, "p3.ln2")
            ln_apply2(fS[:, 2048:4096], WST[sl][:], [r_wst[sl]], r_stage[1],
                      sl, "p3.ln2")
            S.op("vector", lambda E_: E_.tensor_add(
                fS[:, 2048:4096], fS[:, 0:2048], fS[:, 2048:4096]),
                reads=[r_stage[0], r_stage[1]], writes=[r_stage[1]], tag="p3.x2")
            S.op("gpsimd", lambda E_, tb=tb: E_.dma_start(
                out=x2_s[cs(tb, 128), :], in_=fS[:, 2048:4096]),
                reads=[r_stage[1]], par_writes=[r_x2s], dma=True, tag="p3.spx2")
            ln_stats2([(fS[:, 2048:4096], 4)], [r_stage[1]], sl, "p3.ln3")
            ln_apply2(fS[:, 2048:4096], fS[:, 2048:4096], [r_stage[1]],
                      r_stage[1], sl, "p3.ln3")
            for db in range(DB):
                pp_t = [POq[0], POq[1]][db % 2]
                pr_t = [r_pO[0], r_pO[1]][db % 2]
                S.op("tensor", lambda E_, db=db, pp=pp_t: E_.transpose(
                    pp[:, 0:128], fS[:, 2048 + db * 128:2048 + (db + 1) * 128],
                    id_f[:]),
                    reads=[r_stage[1], r_const], writes=[pr_t], tag="p3.t")
                S.op("scalar", lambda E_, db=db, tb=tb, pp=pp_t: E_.copy(
                    big16[:, db * 2048 + tb * 128:db * 2048 + (tb + 1) * 128],
                    pp[:, 0:128]),
                    reads=[pr_t], par_writes=[r_xn2], tag="p3.te")
            lg_ps, lg_r = POq[2], r_pO[2]
            for db in range(DB):
                S.op("tensor", lambda E_, db=db, tb=tb, pp=lg_ps: E_.matmul(
                    pp[:, 0:E],
                    big16[:, db * 2048 + tb * 128:db * 2048 + (tb + 1) * 128],
                    gwT[:, cs(db, E)], start=(db == 0), stop=(db == DB - 1)),
                    reads=[r_xn2, r_gwT],
                    writes=[lg_r] if db == 0 else [],
                    par_writes=[] if db == 0 else [lg_r], tag="p3.lg")
            S.op("scalar", lambda E_, pp=lg_ps: E_.activation(
                fT[:, 64:80], pp[:, 0:E], AF.Sigmoid),
                reads=[lg_r], writes=[r_fT], tag="p3.sig")
            S.op("vector", lambda E_: E_.tensor_add(
                fT[:, 80:96], fT[:, 64:80], ebias_t[:]),
                reads=[r_fT, r_const], writes=[r_fT], tag="p3.rb")
            S.op("vector", lambda E_: E_.tensor_copy(fT[:, 96:112], fT[:, 80:96]),
                 reads=[r_fT], writes=[r_fT], tag="p3.rc")
            for k in range(3):
                S.op("vector", lambda E_: E_.reduce_max(
                    fT[:, 112:113], fT[:, 96:112], axis=AX.X),
                    reads=[r_fT], writes=[r_fT], tag="p3.rm")
                S.op("vector", lambda E_: E_.tensor_single_scalar(
                    fT[:, 128:144], fT[:, 96:112], fT[:, 112:113], ALU.is_equal),
                    reads=[r_fT], writes=[r_fT], tag="p3.req")
                S.op("vector", lambda E_: E_.scalar_tensor_tensor(
                    fT[:, 96:112], fT[:, 128:144], NEG, fT[:, 96:112],
                    ALU.mult, ALU.add),
                    reads=[r_fT], writes=[r_fT], tag="p3.rel")
            S.op("vector", lambda E_: E_.reduce_max(
                fT[:, 112:113], fT[:, 96:112], axis=AX.X),
                reads=[r_fT], writes=[r_fT], tag="p3.rm4")
            S.op("vector", lambda E_: E_.tensor_single_scalar(
                fT[:, 128:144], fT[:, 80:96], fT[:, 112:113], ALU.is_ge),
                reads=[r_fT], writes=[r_fT], tag="p3.rge")
            S.op("vector", lambda E_: E_.tensor_mul(
                fT[:, 96:112], fT[:, 64:80], fT[:, 128:144]),
                reads=[r_fT], writes=[r_fT], tag="p3.rw")
            S.op("vector", lambda E_: E_.reduce_sum(
                fT[:, 113:114], fT[:, 96:112], axis=AX.X),
                reads=[r_fT], writes=[r_fT], tag="p3.rs")
            S.op("vector", lambda E_: E_.reciprocal(fT[:, 114:115], fT[:, 113:114]),
                 reads=[r_fT], writes=[r_fT], tag="p3.rrc")
            S.op("vector", lambda E_, tb=tb: E_.tensor_scalar_mul(
                cw_all[:, cs(tb, E)], fT[:, 96:112], fT[:, 114:115]),
                reads=[r_fT], par_writes=[r_cw], tag="p3.cw")
        if "xn2T" in tap_outs:
            tap_copy("xn2T", [(big16[:, cs(db, 2048)], 128, 2048)
                              for db in range(DB)], [r_xn2])
        if "cw" in tap_outs:
            tap_copy("cw", [(cw_all[:], 128, TB * E)], [r_cw])

        r_wgup = [S.alias(r_wo, name=f"wgup{i}") for i in range(4)]
        r_hT2 = [S.alias(r_ckvA, name="hT0"), S.alias(r_ckvA, name="hT1")]
        r_silu = [S.alias(r_stage[0], name=f"silu{i}") for i in range(4)]
        SILU = lambda i: fSb[:, i * 512:(i + 1) * 512]
        HT = lambda ibg: ckvA[:, ibg * 2048:(ibg + 1) * 2048]

        for e in range(E + 1):
            for ih in range(2):
                k2 = (2 * e + ih) % 2
                gsl, usl = 2 * k2, 2 * k2 + 1
                for db in range(DB):
                    S.op("sync", lambda E_, e=e, ih=ih, db=db, gsl=gsl:
                         E_.dma_start_transpose(
                             wA[:, WGUP(gsl) + db * 512:WGUP(gsl) + (db + 1) * 512],
                             wg_s[e][cs(ih, 512), cs(db, 128)]),
                         reads=[R_cast[f"wg{e}"]], par_writes=[r_wgup[gsl]],
                         dma=True, tag="p4.lwg")
                    S.op("sync", lambda E_, e=e, ih=ih, db=db, usl=usl:
                         E_.dma_start_transpose(
                             wA[:, WGUP(usl) + db * 512:WGUP(usl) + (db + 1) * 512],
                             wu_s[e][cs(ih, 512), cs(db, 128)]),
                         reads=[R_cast[f"wu{e}"]], par_writes=[r_wgup[usl]],
                         dma=True, tag="p4.lwu")
                for ib in range(4):
                    ibg = ih * 4 + ib
                    for db in range(DB):
                        for tc in range(4):
                            S.op("tensor", lambda E_, gsl=gsl, db=db, ib=ib,
                                 tc=tc, pp=pp_cyc[tc]: E_.matmul(
                                     pp[:],
                                     wA[:, WGUP(gsl) + db * 512 + ib * 128:
                                        WGUP(gsl) + db * 512 + (ib + 1) * 128],
                                     big16[:, db * 2048 + tc * 512:
                                           db * 2048 + (tc + 1) * 512],
                                     start=(db == 0), stop=(db == DB - 1)),
                                 reads=[r_wgup[gsl], r_xn2],
                                 writes=[pr_cyc[tc]] if db == 0 else [],
                                 par_writes=[] if db == 0 else [pr_cyc[tc]],
                                 tag="p4.g")
                    for tc in range(4):
                        S.op("scalar", lambda E_, pp=pp_cyc[tc], ssl=tc:
                             E_.activation(SILU(ssl), pp[:], AF.Silu),
                             reads=[pr_cyc[tc]], writes=[r_silu[tc]],
                             tag="p4.silu")
                    for db in range(DB):
                        for tc in range(4):
                            S.op("tensor", lambda E_, usl=usl, db=db, ib=ib,
                                 tc=tc, pp=POq[tc]: E_.matmul(
                                     pp,
                                     wA[:, WGUP(usl) + db * 512 + ib * 128:
                                        WGUP(usl) + db * 512 + (ib + 1) * 128],
                                     big16[:, db * 2048 + tc * 512:
                                           db * 2048 + (tc + 1) * 512],
                                     start=(db == 0), stop=(db == DB - 1)),
                                 reads=[r_wgup[usl], r_xn2],
                                 writes=[r_pO[tc]] if db == 0 else [],
                                 par_writes=[] if db == 0 else [r_pO[tc]],
                                 tag="p4.u")
                    for tc in range(4):
                        S.op("vector", lambda E_, ibg=ibg, tc=tc, pp=POq[tc]:
                             E_.tensor_tensor(
                                 HT(ibg)[:, cs(tc, 512)], pp, SILU(tc),
                                 ALU.mult),
                            reads=[r_pO[tc], r_silu[tc]], par_writes=[r_hT2[ih]],
                            tag="p4.h")
                S.op("gpsimd", lambda E_, e=e, ih=ih: E_.dma_start(
                    out=hT_s[e][cs(ih, 512), :].rearrange(
                        "(ib p) t -> p ib t", p=128),
                    in_=ckvA[:, ih * 8192:(ih + 1) * 8192].rearrange(
                        "p (ib t) -> p ib t", ib=4)),
                    reads=[r_hT2[ih]], par_writes=[r_hTs[e]], dma=True,
                    tag="p4.sp")

        if "hT2" in tap_outs:
            S.op("gpsimd", lambda E_: E_.dma_start(
                out=tap_outs["hT2"][:], in_=hT_s[2][:]),
                reads=[r_hTs[2]], par_writes=[r_out], dma=True, tag="tap.h2")
        if "hTs" in tap_outs:
            S.op("gpsimd", lambda E_: E_.dma_start(
                out=tap_outs["hTs"][:], in_=hT_s[E][:]),
                reads=[r_hTs[E]], par_writes=[r_out], dma=True, tag="tap.hs")

        r_wdt = [S.alias(r_qlB, *r_wst, name="wdtA"),
                 S.alias(*r_wgup, name="wdtB")]
        r_hst = [S.alias(*r_hT2, name="hstA"),
                 S.alias(*r_wgup, name="hstB")]
        r_acc = S.alias(r_avt, r_xn2, name="acc")
        WDTs = [qlB[:, 0:8192], WDT]
        HSTs = [ckvA[:, 0:16384], HSTB]

        for dh in range(2):
            for e in range(E + 1):
                sl = e % 2
                for ib in range(IB):
                    S.op("sync", lambda E_, e=e, dh=dh, ib=ib, sl=sl:
                         E_.dma_start_transpose(
                             WDTs[sl][:, cs(ib, 1024)],
                             wd_s[e][cs(dh, 1024), cs(ib, 128)]),
                         reads=[R_cast[f"wd{e}"]], par_writes=[r_wdt[sl]],
                         dma=True, tag="p5.lwd")
                S.op("scalar", lambda E_, e=e, sl=sl: E_.dma_start(
                    out=HSTs[sl][:].rearrange("p (ib t) -> p ib t", ib=8),
                    in_=hT_s[e][:].rearrange("(ib p) t -> p ib t", p=128)),
                    reads=[r_hTs[e]], writes=[r_hst[sl]], dma=True, tag="p5.lh")
                for tb in range(TB):
                    for dc in range(2):
                        k = tb * 2 + dc
                        pr, pp = pr_cyc[k % 4], pp_cyc[k % 4]
                        for ib in range(IB):
                            S.op("tensor", lambda E_, sl=sl, ib=ib, tb=tb,
                                 dc=dc, pp=pp: E_.matmul(
                                     pp[:],
                                     HSTs[sl][:, ib * 2048 + tb * 128:
                                              ib * 2048 + (tb + 1) * 128],
                                     WDTs[sl][:, ib * 1024 + dc * 512:
                                              ib * 1024 + (dc + 1) * 512],
                                     start=(ib == 0), stop=(ib == IB - 1)),
                                 reads=[r_hst[sl], r_wdt[sl]], writes=[pr],
                                 tag="p5.d")
                        acc_sl = accF[:, tb * 1024 + dc * 512:
                                      tb * 1024 + (dc + 1) * 512]
                        if e == 0:
                            S.op("vector", lambda E_, a=acc_sl, tb=tb, e=e,
                                 pp=pp: E_.tensor_single_scalar(
                                     a, pp[:], cw_all[:, tb * E + e:tb * E + e + 1],
                                     ALU.mult),
                                 reads=[pr, r_cw], par_writes=[r_acc],
                                 tag="p5.a0")
                        elif e < E:
                            S.op("vector", lambda E_, a=acc_sl, tb=tb, e=e,
                                 pp=pp: E_.scalar_tensor_tensor(
                                     a, pp[:], cw_all[:, tb * E + e:tb * E + e + 1],
                                     a, ALU.mult, ALU.add),
                                 reads=[pr, r_cw, r_acc], par_writes=[r_acc],
                                 tag="p5.ae")
                        else:
                            S.op("vector", lambda E_, a=acc_sl, pp=pp:
                                 E_.tensor_tensor(a, pp[:], a, ALU.add),
                                 reads=[pr, r_acc], par_writes=[r_acc],
                                 tag="p5.as")
            if dh == 0:
                S.op("gpsimd", lambda E_: E_.dma_start(
                    out=y0_s[:].rearrange("(tb p) d -> p tb d", p=128),
                    in_=accF[:].rearrange("p (tb d) -> p tb d", tb=16)),
                    reads=[r_acc], par_writes=[r_y0s], dma=True, tag="p5.spy")
                r_acc = S.alias(r_acc, r_y0s, name="acc1")

        if "y0" in tap_outs:
            S.op("sync", lambda E_: E_.dma_start(
                out=tap_outs["y0"][:], in_=y0_s[:]),
                reads=[r_y0s], par_writes=[r_out], dma=True, tag="tap.y0")
        if "acc" in tap_outs:
            S.op("sync", lambda E_: E_.dma_start(
                out=tap_outs["acc"][:], in_=accF[:]),
                reads=[r_acc], par_writes=[r_out], dma=True, tag="tap.acc")

        r_p6 = [S.alias(r_stage[0], r_stage[1], name="p6a"),
                S.alias(r_wdt[0], name="p6b")]
        P6B = [(fS[:, 0:1024], fS[:, 1024:2048], fS[:, 2048:4096],
                fS[:, 0:2048]),
               (fQ[:, 0:1024], fQ[:, 1024:2048], fQ[:, 2048:4096],
                fQ[:, 0:2048])]
        for tb in range(TB):
            sl = tb % 2
            ybuf, zbuf, xbuf, yzbuf = P6B[sl]
            rb6 = r_p6[sl]
            load(ybuf, y0_s[cs(tb, 128), :], rb6, tag="p6.ly", par=False)
            load(xbuf, x2_s[cs(tb, 128), :], rb6, q="scalar", tag="p6.lx2",
                 par=True)
            acc_t = accF[:, cs(tb, 1024)]
            ln_stats2([(ybuf, 2), (acc_t, 2)], [rb6, r_acc], sl, "p6.ln4")
            ln_apply2(ybuf, ybuf, [rb6], rb6, sl, "p6.af0", par=True)
            ln_apply2(zbuf, acc_t, [r_acc], rb6, sl, "p6.af1", par=True)
            S.op("vector", lambda E_, x=xbuf, yz=yzbuf: E_.tensor_add(
                x, x, yz),
                reads=[rb6], writes=[rb6], tag="p6.add")
            S.op("sync", lambda E_, tb=tb, x=xbuf: E_.dma_start(
                out=out[cs(tb, 128), :], in_=x),
                reads=[rb6], par_writes=[r_out], dma=True, tag="p6.out")

        S.finalize()
        with nc.Block() as block:
            S.emit(block, sems)
    return nc


def _rope_tables(rd, max_t, theta=10000.0):
    freqs = 1.0 / theta ** (np.arange(0, rd, 2, dtype=np.float32) / rd)
    t = np.arange(max_t, dtype=np.float32)
    ang = np.outer(t, freqs)
    cos = np.tile(np.cos(ang), (1, 2))
    sin = np.tile(np.sin(ang), (1, 2))
    return cos, sin


def make_core_inputs(full, c):
    b, p = c // 2, c % 2
    f32 = np.float32
    bf = ml_dtypes.bfloat16
    cos, sin = _rope_tables(RD, T)
    scale = f32(1.0) / np.sqrt(f32(HD + RD))

    def fold(s):
        o = s.T.copy()
        o[0:32] *= -1.0
        return o

    sl = np.arange(128)[:, None]
    tl = np.arange(512)[None, :]
    m = np.empty((128, 8 * 512), dtype=np.float32)
    for rb in range(8):
        m[:, rb * 512:(rb + 1) * 512] = np.where(
            (128 * rb + sl) <= (2 * tl + p), 0.0, NEG)
    return {
        "x_b": np.ascontiguousarray(full["L0_attn_x"][b]),
        "x_q": np.ascontiguousarray(full["L0_attn_x"][b, p::2]),
        "wqd": full["L0_attn_Wqd"], "wqu": full["L0_attn_Wqu"],
        "wqr": full["L0_attn_Wqr"], "wkvd": full["L0_attn_Wkvd"],
        "wku": full["L0_attn_Wku"], "wvu": full["L0_attn_Wvu"],
        "wkr": full["L0_attn_Wkr"], "wo": full["L0_attn_Wo"],
        "gate_w": full["L0_ffn_gate_w"],
        "ebias_bc": np.ascontiguousarray(
            np.tile(full["L0_ffn_expert_bias"][None, :], (128, 1))),
        "w_gate": full["L0_ffn_w_gate"], "w_up": full["L0_ffn_w_up"],
        "w_down": full["L0_ffn_w_down"],
        "sw_gate": full["L0_ffn_sw_gate"], "sw_up": full["L0_ffn_sw_up"],
        "sw_down": full["L0_ffn_sw_down"],
        "cosq": np.ascontiguousarray(cos.T[:, p::2] * scale).astype(bf),
        "sinq": np.ascontiguousarray(fold(sin)[:, p::2] * scale).astype(bf),
        "cosk": np.ascontiguousarray(cos.T).astype(bf),
        "sink": np.ascontiguousarray(fold(sin)).astype(bf),
        "maskin": m.astype(bf),
        "ident": np.eye(128, dtype=np.float32),
        "ones_col": np.ones((128, 1), dtype=bf),
        "ones_row": np.ones((1, 128), dtype=bf),
    }


def kernel(**inputs):
    nc = build_program()
    in_maps = [make_core_inputs(inputs, c) for c in range(8)]
    res = run_bass_kernel_spmd(nc, in_maps, core_ids=list(range(8)))
    out = np.empty((B, T, D), dtype=np.float32)
    for c in range(8):
        b, p = c // 2, c % 2
        out[b, p::2] = res.results[c]["out"]
    return out
